# version 01
# baseline (speedup 1.0000x reference)
from contextlib import ExitStack

import numpy as np
import ml_dtypes

import concourse.bass as bass
import concourse.mybir as mybir
from concourse.bass_utils import run_bass_kernel_spmd

F32 = mybir.dt.float32
BF16 = mybir.dt.bfloat16
AXX = mybir.AxisListType.X
ALU = mybir.AluOpType
ACTF = mybir.ActivationFunctionType

NC = 8
D = 2048
S = 8192
H = 16
DF = 8192
TPC = 1024
NT = 8
KD = 16
KF = 64
MAGIC = float(np.float32(1.5 * 2 ** 23))
INV_SQRT_HD = float(np.float32(1.0) / np.float32(np.sqrt(np.float32(128.0))))
EXP_SHIFT = -20.0
WM_SL = 12 * 128
NBLK = 32
WQP = 3 * D * TPC
KVP = H * 128 * TPC + NT * 128 * D
VOFF = H * 128 * TPC
W_CNT = float(DF * D)


def tiles_of_core(c):
    return [8 * j + (c if j % 2 == 0 else 7 - c) for j in range(NT)]


def owner_slot(t):
    j, m = divmod(t, 8)
    return (m if j % 2 == 0 else 7 - m), j


class Plan:
    def __init__(self):
        self.ops = {k: [] for k in ("sync", "tensor", "vector", "scalar", "gpsimd")}
        from collections import defaultdict
        self.cnt = defaultdict(int)

    AUTO = {"vector": "V", "scalar": "S"}

    def op(self, eng, build_fn, waits=(), inc=None, amt=1, auto=True):
        aut = self.AUTO.get(eng) if auto else None
        w = list(waits)
        if aut is not None:
            if self.cnt[aut] > 0:
                w.append((aut, self.cnt[aut]))
            if inc is None:
                inc = aut
            assert inc == aut
        w = tuple((s, v) for (s, v) in w if v and v > 0)
        if inc is not None:
            self.cnt[inc] += amt
        post = self.cnt[inc] if inc is not None else None

        def fn(e, sems, _w=w, _b=build_fn, _i=inc, _a=amt):
            for sname, v in _w:
                e.wait_ge(sems[sname], v)
            ins = _b(e)
            if _i is not None:
                ins.then_inc(sems[_i], _a)

        self.ops[eng].append(fn)
        return post

    def wait(self, eng, waits):
        self.op(eng, lambda e: None, waits=waits, auto=False)

    def c(self, sem):
        return self.cnt.get(sem, 0) if hasattr(self.cnt, "get") else self.cnt[sem]


class Arena:
    def __init__(self, nc, ctx, nbytes):
        self.nc = nc
        self.nbytes = nbytes
        self.h = ctx.enter_context(nc.sbuf_tensor("dynarena", [128, nbytes], mybir.dt.uint8))
        self.base = nc.lookup_mloc(self.h).addr
        self.uid = 0

    def at(self, name, off, shape, dt):
        per = 1
        for s in shape[1:]:
            per *= s
        per *= mybir.dt.size(dt)
        assert off % 32 == 0 and off + per <= self.nbytes, (name, off, per)
        self.uid += 1
        h = self.nc.virtual_sbuf_tensor(f"{name}_{self.uid}", shape, dt)
        self.nc._tensor(h.name, h.shape, h.dtype, type="SB")
        m = self.nc.lookup_mloc(h)
        m.allocated = True
        m.addr = self.base + off
        return h


def build(dbg=(), upto="out"):
    dbg = set(dbg)
    PHASES = ["mod", "ln1", "qkv", "attn", "wout", "ln2", "ffn1", "out"]
    upto_i = PHASES.index(upto)

    def want(ph):
        return PHASES.index(ph) <= upto_i

    nc = bass.Bass("TRN2", target_bir_lowering=False)
    pl = Plan()

    def inp(name, shape, dt):
        return nc.dram_tensor(name, shape, dt, kind="ExternalInput").ap()

    x_in = inp("x", [TPC, D], BF16)
    wqkv_in = inp("wqkv", [D, 3 * D], BF16)
    bqkv_in = inp("bqkv", [3 * D], BF16)
    wout_in = inp("wout", [D, D], BF16)
    bout_in = inp("bout", [D], BF16)
    cvec_in = inp("cvec", [D], F32)
    wmod_in = inp("wmods", [D, WM_SL], F32)
    wg_in = inp("wgs", [TPC, D], F32)
    wu_in = inp("wus", [TPC, D], F32)
    wd_in = inp("wds", [D, TPC], F32)
    mask_in = inp("masks", [8 * 8 * 128, 128], BF16)
    ones_bf_in = inp("ones_bf", [128, 128], BF16)
    id_bf_in = inp("id_bf", [128, 128], BF16)
    id_f_in = inp("id_f", [128, 128], F32)

    out_d = nc.dram_tensor("out", [TPC, D], F32, kind="ExternalOutput").ap()

    dbg_outs = {}

    def dbg_def(name, shape, dt=F32):
        if name in dbg:
            dbg_outs[name] = nc.dram_tensor("dbg_" + name, shape, dt, kind="ExternalOutput").ap()

    dbg_def("mod", [128, 96])
    dbg_def("cv", [128, 16])
    dbg_def("sil", [128, 16])
    dbg_def("msl", [128, 12])
    dbg_def("dv", [128, 64])
    dbg_def("h", [128, KD * TPC], BF16)
    dbg_def("q", [128, H * TPC], BF16)
    dbg_def("kv", [KVP], BF16)
    dbg_def("wqa", [NC * WQP], BF16)
    dbg_def("scl", [1, 8])
    dbg_def("att", [128, H * TPC])
    dbg_def("lsl", [1, H * TPC])
    dbg_def("x1", [128, KD * TPC])
    dbg_def("xq", [128, KD * TPC], BF16)
    dbg_def("s2", [128, 16])
    dbg_def("mid", [KF * 128, TPC])

    mod_sl_d = nc.dram_tensor("mod_sl", [1, WM_SL], F32)
    mod_ag_d = nc.dram_tensor("mod_ag", [NC, WM_SL], F32, addr_space="Shared")
    scl_sl_d = nc.dram_tensor("scl_sl", [1, 4], F32)
    scl_ar_d = nc.dram_tensor("scl_ar", [1, 4], F32, addr_space="Shared")
    wq_sl_d = nc.dram_tensor("wq_sl", [WQP], BF16)
    wq_ag_d = nc.dram_tensor("wq_ag", [NC, WQP], BF16, addr_space="Shared")
    kv_sl_d = nc.dram_tensor("kv_sl", [KVP], BF16)
    kv_ag_d = nc.dram_tensor("kv_ag", [NC, KVP], BF16, addr_space="Shared")
    mid_d = nc.dram_tensor("mid", [KF, 128, TPC], F32)
    midq_d = nc.dram_tensor("midq", [KF, 128, TPC], BF16)

    perm = ExitStack()

    def sbp(name, shape, dt):
        return perm.enter_context(nc.sbuf_tensor(name + "_sb", shape, dt))

    ones_bf = sbp("ones_bf", [128, 128], BF16)
    id_bf = sbp("id_bf", [128, 128], BF16)
    id_f = sbp("id_f", [128, 128], F32)
    csc = sbp("csc", [128, 4], F32)
    cv = sbp("cv", [128, 16], F32)
    sil = sbp("sil", [128, 16], F32)
    t16a = sbp("t16a", [128, 16], F32)
    t16b = sbp("t16b", [128, 16], F32)
    modv = sbp("modv", [128, 96], F32)
    dv = sbp("dv", [128, 64], F32)
    bqkv = sbp("bqkv", [128, 48], F32)
    boutv = sbp("boutv", [128, 16], F32)
    sclv = sbp("sclv", [1, 8], F32)
    sclc = sbp("sclc", [128, 8], F32)
    accv = sbp("accv", [128, 4], F32)
    wqacc = sbp("wqacc", [128, 4], F32)
    stok = sbp("stok", [128, 16], F32)
    stokT = sbp("stokT", [16, 128], F32)

    AR_BYTES = 206848
    ar = Arena(nc, perm, AR_BYTES)

    q_fm = ar.at("q_fm", 0, [128, H * TPC], BF16)
    x_fm = ar.at("x_fm", 32768, [128, KD * TPC], BF16)
    h_fm = ar.at("h_fm", 65536, [128, KD * TPC], BF16)
    v_sl = ar.at("v_sl", 98304, [128, H * TPC], BF16)
    wbuf = ar.at("wbuf", 131072, [128, 2 * 16 * 256], BF16)
    kvtmp = ar.at("kvtmp", 147456, [128, 4 * TPC], BF16)
    sqbuf = ar.at("sqbuf", 147456, [128, TPC], BF16)
    wfbuf = ar.at("wfbuf", 155648, [128, 2 * 2048], F32)
    wmodbuf = ar.at("wmodbuf", 155648, [128, 2 * WM_SL], F32)
    mtile = ar.at("mtile", 172032, [128, 2048], BF16)
    mttile = ar.at("mttile", 176128, [128, 2048], BF16)
    st1 = ar.at("st1", 180224, [1, TPC], F32)
    st2 = ar.at("st2", 184320, [1, TPC], F32)
    st3 = ar.at("st3", 188416, [1, TPC], F32)
    abc = ar.at("abc", 192512, [128, TPC], F32)
    bbc = ar.at("bbc", 196608, [128, TPC], F32)
    vsc1 = ar.at("vsc1", 200704, [128, TPC], F32)
    onesf_col = ar.at("onesf_col", 204800, [128, 128], F32)

    att = ar.at("att", 32768, [128, H * TPC], F32)
    lsl = ar.at("lsl", 98304, [1, H * TPC], F32)
    kblk = ar.at("kblk", 163840, [128, 2 * H * 256], BF16)
    vblk = ar.at("vblk", 180224, [128, 2 * 2 * D], BF16)
    ptb = ar.at("ptb", 196608, [128, 2 * 1024], BF16)
    mskb = ar.at("mskb", 200704, [128, 8 * 128], BF16)

    x_rl = ar.at("x_rl", 0, [128, KD * TPC], BF16)
    att_bf = ar.at("att_bf", 98304, [128, H * TPC], BF16)
    x1 = ar.at("x1", 32768, [128, KD * TPC], F32)
    woutbuf = ar.at("woutbuf", 163840, [128, 2 * 16 * 256], BF16)
    csc2 = ar.at("csc2", 180224, [128, TPC], F32)

    xq = ar.at("xq", 0, [128, KD * TPC], BF16)
    h2sl = ar.at("h2sl", 98304, [128, KD * TPC], F32)
    x1b = ar.at("x1b", 172032, [128, TPC], BF16)
    sq2 = ar.at("sq2", 174080, [128, TPC], BF16)
    mmax = ar.at("mmax", 163840, [128, TPC], F32)
    mmin = ar.at("mmin", 167936, [128, TPC], F32)
    s2bc = ar.at("s2bc", 176128, [128, TPC], F32)
    dsc = ar.at("dsc", 200704, [128, TPC], F32)

    wslab = ar.at("wslab", 98304, [128, 4 * 16 * 512], BF16)
    ubg = ar.at("ubg", 163840, [128, TPC], F32)
    ubu = ar.at("ubu", 167936, [128, TPC], F32)
    ubd = ar.at("ubd", 172032, [128, TPC], F32)
    midb = ar.at("midb", 180224, [128, 4 * TPC], F32)
    gsc1 = ar.at("gsc1", 196608, [128, TPC], F32)
    gsc2 = ar.at("gsc2", 200704, [128, TPC], F32)
    fmx = ar.at("fmx", 172032, [128, TPC], F32)
    fmn = ar.at("fmn", 176128, [128, TPC], F32)
    s2rowA = ar.at("s2rowA", 176128, [1, 1024], F32)
    s2rowB = ar.at("s2rowB", 180224, [1, 1024], F32)
    modrow = ar.at("modrow", 65536, [1, NC * WM_SL], F32)

    wdslab = ar.at("wdslab", 0, [128, 2 * 8 * 512], BF16)
    midqb = ar.at("midqb", 16384, [128, 2 * TPC], BF16)
    fsc1 = ar.at("fsc1", 20480, [128, TPC], F32)
    fsc2 = ar.at("fsc2", 24576, [128, TPC], F32)
    x2tm = ar.at("x2tm", 98304, [128, NT * D], F32)

    pp = [perm.enter_context(nc.psum_tensor(f"pp{i}", [128, 1024], F32)) for i in range(4)]
    pA, pB, pC, pD = pp

    def ldma(dst, src, waits=(), transpose=False, slow=False, sem="Lc"):
        return pl.op("sync", lambda e, d=dst, s=src, t=transpose, sl=slow: e.dma_start(
            out=d, in_=s, transpose=t, allow_slow_non_contiguous=sl),
                     waits=waits, inc=sem, amt=16)

    def edma(dst, src, waits=(), sem="Od"):
        return pl.op("sync", lambda e, d=dst, s=src: e.dma_start(out=d, in_=s),
                     waits=waits, inc=sem, amt=16)

    def fdma(dst, src, waits=(), sem="Fo"):
        return pl.op("scalar", lambda e, d=dst, s=src: e.dma_start(out=d, in_=s),
                     waits=waits, inc=sem, amt=16, auto=False)

    def gdma(dst, src, waits=(), slow=False, sem="Gd"):
        return pl.op("gpsimd", lambda e, d=dst, s=src, sl=slow: e.dma_start(
            out=d, in_=s, allow_slow_non_contiguous=sl),
                     waits=waits, inc=sem, amt=16)

    def vop(fn, waits=(), inc=None):
        return pl.op("vector", fn, waits=waits)

    def vop_s(fn, waits=()):
        return pl.op("vector", fn, waits=waits)

    def sop(fn, waits=(), inc="S"):
        return pl.op("scalar", fn, waits=waits)

    def top(fn, waits=(), inc=None):
        return pl.op("tensor", fn, waits=waits, inc=inc)

    def coll(kind, alu, ins_, outs_, waits=(), sem="Cm"):
        return pl.op("gpsimd", lambda e, k=kind, a=alu, i=ins_, o=outs_: e.collective_compute(
            k, a, replica_groups=[list(range(NC))], ins=[i], outs=[o]),
            waits=waits, inc=sem, amt=1)

    def finish():
        waits = [(k, pl.c(k)) for k in ("Oo", "Od", "Gd", "Em", "Ew0", "Ew1", "Ek0", "Ek1",
                                         "Fo0", "Fo1", "Fo2", "Fo3", "Fq0", "Fq1")]
        for eng in ("sync", "gpsimd"):
            pl.wait(eng, waits)
        return _emit(nc, pl, perm)

    ldma(ones_bf[:, :], ones_bf_in[:, :])
    ldma(id_bf[:, :], id_bf_in[:, :])
    ldma(id_f[:, :], id_f_in[:, :])
    ldma(cv[:, :], cvec_in.rearrange("(k p) -> p k", p=128), slow=True)
    gdma(bqkv[:, :], bqkv_in.rearrange("(o p) -> p o", p=128), slow=True, sem="Gb")
    G_bias = gdma(boutv[:, :], bout_in.rearrange("(o p) -> p o", p=128), slow=True, sem="Gb")
    L_consts = pl.c("Lc")

    vop(lambda e: e.memset(csc[:, 0:1], EXP_SHIFT), inc=None)
    vop(lambda e: e.memset(csc[:, 1:2], INV_SQRT_HD), inc=None)
    vop(lambda e: e.memset(onesf_col[:, :], 1.0), inc=None)
    V_csc = vop(lambda e: e.memset(csc[:, 2:3], 0.0))

    L_x = L_consts
    for k in range(KD):
        L_x = ldma(x_fm[:, k * TPC:(k + 1) * TPC], x_in[:, k * 128:(k + 1) * 128], transpose=True, sem="Lx")

    A1, A3, A5, A7, A9 = 0.25, -1.0 / 48, 1.0 / 480, -17.0 / 80640, 31.0 / 1451520
    pl.wait("vector", [("Lc", L_consts), ("Gb", G_bias)])
    vop_s(lambda e: e.tensor_tensor(out=t16a[:, :], in0=cv[:, :], in1=cv[:, :], op=ALU.mult))
    vop_s(lambda e: e.tensor_scalar(out=t16b[:, :], in0=t16a[:, :], scalar1=A9, scalar2=A7,
                                    op0=ALU.mult, op1=ALU.add))
    for coef in (A5, A3, A1):
        vop_s(lambda e: e.tensor_tensor(out=t16b[:, :], in0=t16a[:, :], in1=t16b[:, :], op=ALU.mult))
        vop_s(lambda e, cf=coef: e.tensor_scalar(out=t16b[:, :], in0=t16b[:, :], scalar1=cf, scalar2=None,
                                                 op0=ALU.add))
    vop_s(lambda e: e.tensor_tensor(out=t16b[:, :], in0=cv[:, :], in1=t16b[:, :], op=ALU.mult))
    vop_s(lambda e: e.tensor_scalar(out=t16b[:, :], in0=t16b[:, :], scalar1=0.5, scalar2=None, op0=ALU.add))
    V_sil = vop_s(lambda e: e.tensor_tensor(out=sil[:, :], in0=cv[:, :], in1=t16b[:, :], op=ALU.mult))

    for k in range(KD):
        buf = k % 2
        w = [("T", pl.c("T"))] if k >= 2 else []
        Lw = ldma(wmodbuf[:, buf * WM_SL:(buf + 1) * WM_SL], wmod_in[k * 128:(k + 1) * 128, :], waits=w,
                  sem=f"Wm{buf}")
        for jj in range(3):
            out_ap = pA[0:1, jj * 512:(jj + 1) * 512] if jj < 2 else pB[0:1, 0:512]
            top(lambda e, b=buf, jj=jj, k=k, oa=out_ap: e.matmul(
                oa, sil[:, k:k + 1],
                wmodbuf[:, b * WM_SL + jj * 512: b * WM_SL + (jj + 1) * 512],
                start=(k == 0), stop=(k == KD - 1)),
                waits=([(f"Wm{buf}", Lw), ("V", V_sil)] if jj == 0 else ()),
                inc=("T" if jj == 2 else None))
    T_mod = pl.c("T")
    vop(lambda e: e.tensor_copy(out=st1[0:1, 0:1024], in_=pA[0:1, 0:1024]), waits=[("T", T_mod)], inc=None)
    V_modev = vop(lambda e: e.tensor_copy(out=st1[0:1, 1024:1536] if False else st2[0:1, 0:512], in_=pB[0:1, 0:512]))
    E_mod = edma(mod_sl_d.ap()[0:1, 0:1024], st1[0:1, 0:1024], waits=[("V", V_modev)], sem="Em")
    E_mod = edma(mod_sl_d.ap()[0:1, 1024:1536], st2[0:1, 0:512], waits=[("V", V_modev)], sem="Em")
    C_mod = coll("AllGather", ALU.bypass, mod_sl_d.ap(), mod_ag_d.ap(), waits=[("Em", E_mod)], sem="Cm")
    L_mod = ldma(modrow[0:1, :], mod_ag_d.ap().rearrange("r w -> (r w)").unsqueeze(0)
                 if hasattr(mod_ag_d.ap(), "unsqueeze") else mod_ag_d.ap().rearrange("r w -> (r w)"),
                 waits=[("Cm", C_mod)], sem="Lc")
    for j in range(96):
        pt = pA if j < 48 else pB
        top(lambda e, j=j, pt=pt: e.transpose(pt[:, (j % 48):(j % 48) + 1], modrow[0:1, j * 128:(j + 1) * 128],
                                              id_f[0:1, 0:1]),
            waits=([("Lc", L_mod)] if j == 0 else ()), inc=("T" if j == 95 else None))
    T_modT = pl.c("T")
    vop(lambda e: e.tensor_copy(out=modv[:, 0:48], in_=pA[:, 0:48]), waits=[("T", T_modT)], inc=None)
    L_mod = vop(lambda e: e.tensor_copy(out=modv[:, 48:96], in_=pB[:, 0:48]))
    pl.wait("vector", [("V", L_mod)])
    vop_s(lambda e: e.tensor_scalar(out=dv[:, 0:16], in0=modv[:, 16:32], scalar1=1.0, scalar2=None, op0=ALU.add))
    vop_s(lambda e: e.tensor_scalar(out=dv[:, 16:32], in0=modv[:, 64:80], scalar1=1.0, scalar2=None, op0=ALU.add))
    V_dv = 0
    for (dst, src) in ((32, 32), (48, 80)):
        S_abs = sop(lambda e, s=src: e.activation(out=t16b[:, :], in_=modv[:, s:s + 16], func=ACTF.Abs),
                    waits=[("V", max(L_mod, pl.c("V")))])
        vop_s(lambda e, s=src: e.tensor_scalar(out=t16a[:, :], in0=modv[:, s:s + 16], scalar1=0.0, scalar2=2.0,
                                               op0=ALU.is_ge, op1=ALU.mult))
        vop_s(lambda e: e.tensor_scalar(out=t16a[:, :], in0=t16a[:, :], scalar1=-1.0, scalar2=None, op0=ALU.add))
        vop_s(lambda e: e.tensor_scalar(out=t16b[:, :], in0=t16b[:, :], scalar1=0.25, scalar2=None, op0=ALU.max),
              waits=[("S", S_abs)])
        V_dv = vop_s(lambda e, d=dst: e.tensor_tensor(out=dv[:, d:d + 16], in0=t16a[:, :], in1=t16b[:, :], op=ALU.mult))

    if "cv" in dbg:
        edma(dbg_outs["cv"][:, :], cv[:, :], waits=[("V", V_sil)])
    if "sil" in dbg:
        edma(dbg_outs["sil"][:, :], sil[:, :], waits=[("V", V_sil)])
    if "msl" in dbg:
        edma(dbg_outs["msl"][:, :], t16a[:, 0:12], waits=[("V", V_modev), ("Em", E_mod)])
    if "mod" in dbg:
        edma(dbg_outs["mod"][:, :], modv[:, :], waits=[("V", V_dv)])
    if "dv" in dbg:
        edma(dbg_outs["dv"][:, :], dv[:, :], waits=[("V", V_dv)])
    if not want("ln1"):
        return finish()

    for k in range(KD):
        S_sq = sop(lambda e, k=k: e.activation(out=sqbuf[:, :], in_=x_fm[:, k * TPC:(k + 1) * TPC],
                                               func=ACTF.Square),
                   waits=([("Lx", L_x)] if k == 0 else [("T", pl.c("T"))]))
        for hf in range(2):
            top(lambda e, k=k, hf=hf: e.matmul(
                pB[0:1, hf * 512:(hf + 1) * 512], ones_bf[:, 0:1],
                x_fm[:, k * TPC + hf * 512: k * TPC + hf * 512 + 512],
                start=(k == 0), stop=(k == KD - 1)),
                waits=([("Lx", L_x), ("V", L_mod)] if (k == 0 and hf == 0) else ()))
            top(lambda e, k=k, hf=hf: e.matmul(
                pC[0:1, hf * 512:(hf + 1) * 512], ones_bf[:, 0:1],
                sqbuf[:, hf * 512: hf * 512 + 512],
                start=(k == 0), stop=(k == KD - 1)),
                waits=([("S", S_sq)] if hf == 0 else ()),
                inc=("T" if hf == 1 else None))
    T_ln1 = pl.c("T")
    vop(lambda e: e.tensor_scalar(out=st1[:, :], in0=pB[0:1, :], scalar1=1.0 / D, scalar2=None, op0=ALU.mult),
        waits=[("T", T_ln1)], inc=None)
    vop(lambda e: e.tensor_scalar(out=st2[:, :], in0=pC[0:1, :], scalar1=1.0 / D, scalar2=None, op0=ALU.mult), inc=None)
    vop(lambda e: e.tensor_tensor(out=st3[:, :], in0=st1[:, :], in1=st1[:, :], op=ALU.mult), inc=None)
    vop(lambda e: e.tensor_tensor(out=st2[:, :], in0=st2[:, :], in1=st3[:, :], op=ALU.subtract), inc=None)
    V_var = vop(lambda e: e.tensor_scalar(out=st2[:, :], in0=st2[:, :], scalar1=1e-5, scalar2=None, op0=ALU.add))
    S_rst = sop(lambda e: e.activation(out=st2[:, :], in_=st2[:, :], func=ACTF.Sqrt), waits=[("V", V_var)])
    vop(lambda e: e.reciprocal(out=st2[:, :], in_=st2[:, :]), waits=[("S", S_rst)], inc=None)
    V_mr = vop(lambda e: e.tensor_tensor(out=st1[:, :], in0=st1[:, :], in1=st2[:, :], op=ALU.mult))
    for (ix, (dstb, srcv)) in enumerate(((abc, st2), (bbc, st1))):
        for hf in range(2):
            top(lambda e, s=srcv, hf=hf: e.matmul(
                pD[:, hf * 512:(hf + 1) * 512], onesf_col[0:1, 0:128],
                s[0:1, hf * 512: hf * 512 + 512], start=True, stop=True),
                waits=([("V", max(V_mr, pl.c("V"))), ("V", V_csc)] if hf == 0 else ()),
                inc=("T" if hf == 1 else None))
        Tb = pl.c("T")
        vop(lambda e, d=dstb: e.tensor_copy(out=d[:, :], in_=pD[:, :]), waits=[("T", Tb)])
    V_bc1 = pl.c("V")
    V_h = V_bc1
    for k in range(KD):
        vop(lambda e, k=k: e.tensor_tensor(out=vsc1[:, :], in0=x_fm[:, k * TPC:(k + 1) * TPC], in1=abc[:, :],
                                           op=ALU.mult), inc=None)
        vop(lambda e: e.tensor_tensor(out=vsc1[:, :], in0=vsc1[:, :], in1=bbc[:, :], op=ALU.subtract), inc=None)
        V_h = vop(lambda e, k=k: e.tensor_scalar(out=h_fm[:, k * TPC:(k + 1) * TPC], in0=vsc1[:, :],
                                                 scalar1=dv[:, k:k + 1], scalar2=modv[:, k:k + 1],
                                                 op0=ALU.mult, op1=ALU.add))
    if "h" in dbg:
        edma(dbg_outs["h"][:, :], h_fm[:, :], waits=[("V", V_h)])
    if not want("qkv"):
        return finish()

    slices = [(wg_in, 0, 8, 2048), (wu_in, 1, 8, 2048), (wd_in, 2, 16, 1024)]
    gen_state = {}

    def wq_gen():
        vop(lambda e: e.memset(accv[:, :], 0.0), inc=None)
        vmarks = {}
        idx = 0
        for (src, mi, ntile, width) in slices:
            for tt in range(ntile):
                w = [("V", vmarks[idx - 2])] if idx >= 2 else [("T", T_modT)]
                Fq = gdma(wfbuf[:, (idx % 2) * 2048:(idx % 2) * 2048 + width],
                          src[tt * 128:(tt + 1) * 128, :], waits=w, sem=f"Gw{idx % 2}")
                vop(lambda e, i=idx, wd=width: e.reduce_sum(
                    out=wqacc[:, 0:1], in_=wfbuf[:, (i % 2) * 2048:(i % 2) * 2048 + wd],
                    axis=AXX, apply_absolute_value=True), waits=[(f"Gw{idx % 2}", Fq)], inc=None)
                vmarks[idx] = vop(lambda e, mi=mi: e.tensor_tensor(
                    out=accv[:, mi:mi + 1], in0=accv[:, mi:mi + 1], in1=wqacc[:, 0:1], op=ALU.add))
                idx += 1
                yield
        T_ar = top(lambda e: e.matmul(pA[0:1, 0:4], onesf_col[:, 0:1], accv[:, 0:4], start=True, stop=True),
                   waits=[("V", pl.c("V")), ("V", V_csc)], inc="T")
        V_ar = vop(lambda e: e.tensor_copy(out=sclv[:, 0:4], in_=pA[0:1, 0:4]), waits=[("T", T_ar)])
        gen_state["V_pA"] = V_ar
        E_ar = edma(scl_sl_d[:, :], sclv[:, 0:4], waits=[("V", V_ar)], sem="Em")
        C_ar = coll("AllReduce", ALU.add, scl_sl_d.ap(), scl_ar_d.ap(), waits=[("Em", E_ar)], sem="Ca")
        L_ar = ldma(sclv[:, 0:4], scl_ar_d[:, :], waits=[("Ca", C_ar)], sem="Lc")
        yield
        vop_s(lambda e: e.tensor_scalar(out=sclv[:, 4:8], in0=sclv[:, 0:4], scalar1=1.0 / W_CNT, scalar2=1e-5,
                                        op0=ALU.mult, op1=ALU.max), waits=[("Lc", L_ar)])
        vop_s(lambda e: e.reciprocal(out=sclv[:, 0:4], in_=sclv[:, 4:8]))
        vop_s(lambda e: e.tensor_tensor(out=st3[0:1, 0:4], in0=sclv[:, 4:8], in1=sclv[:, 0:4], op=ALU.mult))
        vop_s(lambda e: e.tensor_scalar(out=st3[0:1, 0:4], in0=st3[0:1, 0:4], scalar1=-1.0, scalar2=2.0,
                                        op0=ALU.mult, op1=ALU.add))
        V_scl = vop_s(lambda e: e.tensor_tensor(out=sclv[:, 0:4], in0=sclv[:, 0:4], in1=st3[0:1, 0:4], op=ALU.mult))
        T_sb = top(lambda e: e.matmul(pA[:, 0:8], onesf_col[0:1, 0:128], sclv[0:1, 0:8], start=True, stop=True),
                   waits=[("V", V_scl)], inc="T")
        V_sclc = vop(lambda e: e.tensor_copy(out=sclc[:, 0:8], in_=pA[:, 0:8]), waits=[("T", T_sb)])
        gen_state["V_pA"] = V_sclc
        if "scl" in dbg:
            edma(dbg_outs["scl"][:, :], sclv[:, 0:8], waits=[("V", V_sclc)])
        yield
        emarks = {}
        idx = 0
        for (src, mi, ntile, width) in slices:
            for tt in range(ntile):
                w = [(f"Ew{idx % 2}", emarks[idx - 2])] if idx >= 2 else []
                Fq = gdma(wfbuf[:, (idx % 2) * 2048:(idx % 2) * 2048 + width],
                          src[tt * 128:(tt + 1) * 128, :], waits=w + [("V", V_sclc)], sem=f"Gw{idx % 2}")
                b0 = (idx % 2) * 2048
                vop(lambda e, b0=b0, wd=width, mi=mi: e.tensor_scalar(
                    out=wfbuf[:, b0:b0 + wd], in0=wfbuf[:, b0:b0 + wd],
                    scalar1=sclc[:, mi:mi + 1], scalar2=MAGIC, op0=ALU.mult, op1=ALU.add),
                    waits=[(f"Gw{idx % 2}", Fq)], inc=None)
                vop(lambda e, b0=b0, wd=width: e.tensor_scalar(
                    out=wfbuf[:, b0:b0 + wd], in0=wfbuf[:, b0:b0 + wd],
                    scalar1=-MAGIC, scalar2=None, op0=ALU.add), inc=None)
                V_m = vop(lambda e, b0=b0, wd=width: e.tensor_scalar(
                    out=mtile[:, 0:wd], in0=wfbuf[:, b0:b0 + wd],
                    scalar1=1.0, scalar2=-1.0, op0=ALU.min, op1=ALU.max))
                ntr = width // 128
                for u in range(ntr):
                    top(lambda e, u=u: e.transpose(
                        pC[:, :].bitcast(BF16)[:, u * 128:(u + 1) * 128],
                        mtile[:, u * 128:(u + 1) * 128], id_bf[:, :]),
                        waits=([("V", V_m)] if u == 0 else ()),
                        inc=("T" if u == ntr - 1 else None))
                T_tr = pl.c("T")
                wmt = [("T", T_tr)]
                if idx >= 1:
                    wmt.append((f"Ew{(idx - 1) % 2}", emarks[idx - 1]))
                V_mt = vop(lambda e, wd=width: e.tensor_copy(out=mttile[:, 0:wd],
                                                             in_=pC[:, :].bitcast(BF16)[:, 0:wd]),
                           waits=wmt)
                gen_state["V_pCD"] = V_mt
                gen_state["V_pA"] = max(gen_state.get("V_pA", 0), 0)
                if mi < 2:
                    base = mi * D * TPC
                    emarks[idx] = edma(
                        wq_sl_d.ap()[base:base + D * TPC].rearrange("(dd p f) -> p dd f", p=128, f=TPC)[:, :, tt * 128:(tt + 1) * 128],
                        mttile[:, :].rearrange("p (dd f) -> p dd f", f=128),
                        waits=[("V", V_mt)], sem=f"Ew{idx % 2}")
                else:
                    base = 2 * D * TPC
                    emarks[idx] = edma(
                        wq_sl_d.ap()[base:base + D * TPC].rearrange("(ff p m) -> p ff m", p=128, m=D)[:, :, tt * 128:(tt + 1) * 128],
                        mttile[:, 0:1024].rearrange("p (ff m) -> p ff m", m=128),
                        waits=[("V", V_mt)], sem=f"Ew{idx % 2}")
                idx += 1
                yield
        gen_state["C_wq"] = coll("AllGather", ALU.bypass, wq_sl_d.ap(), wq_ag_d.ap(),
                                 waits=[("Ew0", pl.c("Ew0")), ("Ew1", pl.c("Ew1"))], sem="Cw")
        gen_state["V_wq"] = pl.c("V")
        if "wqa" in dbg:
            gdma(dbg_outs["wqa"][:], wq_ag_d.ap().rearrange("r w -> (r w)"),
                 waits=[("Cw", gen_state["C_wq"])])
        yield

    need_wq = want("ln2") or ("wqa" in dbg) or ("scl" in dbg)
    wq_it = wq_gen() if need_wq else iter(())

    def pump(n):
        for _ in range(n):
            try:
                next(wq_it)
            except StopIteration:
                break

    ogm = {}
    E_kv = {}
    for og in range(24):
        buf = og % 2
        w = [("T", ogm[("T", og - 2)])] if og >= 2 else [("T", T_modT)]
        Lw = ldma(wbuf[:, buf * 4096:(buf + 1) * 4096].rearrange("p (k o) -> p k o", o=256),
                  wqkv_in[:, og * 256:(og + 1) * 256].rearrange("(k p) o -> p k o", p=128),
                  waits=w, sem=f"Wq{buf}")
        ps_pair = (pA, pB) if og % 2 == 0 else (pC, pD)
        tw = [(f"Wq{buf}", Lw), ("V", V_h)]
        if og >= 2:
            tw.append(("V", ogm[("V", og - 2)]))
        tw.append(("V", gen_state.get("V_pA" if og % 2 == 0 else "V_pCD", 0)))
        first = True
        for k in range(KD):
            for oo in range(2):
                tgt = ps_pair[oo]
                for hf in range(2):
                    top(lambda e, b=buf, k=k, oo=oo, hf=hf, tgt=tgt: e.matmul(
                        tgt[:, hf * 512:(hf + 1) * 512],
                        wbuf[:, b * 4096 + k * 256 + oo * 128: b * 4096 + k * 256 + oo * 128 + 128],
                        h_fm[:, k * TPC + hf * 512: k * TPC + hf * 512 + 512],
                        start=(k == 0), stop=(k == KD - 1)),
                        waits=(tw if first else ()),
                        inc=("T" if (k == KD - 1 and oo == 1 and hf == 1) else None))
                    first = False
        ogm[("T", og)] = pl.c("T")
        Tog = pl.c("T")
        for oo in range(2):
            o = og * 2 + oo
            src = ps_pair[oo]
            if o < 16:
                vop(lambda e, o=o, src=src: e.tensor_scalar(
                    out=q_fm[:, o * TPC:(o + 1) * TPC], in0=src[:, :],
                    scalar1=bqkv[:, o:o + 1], scalar2=None, op0=ALU.add),
                    waits=[("T", Tog)])
            elif o < 32:
                hh = o - 16
                kb = hh % 2
                wv = [("T", Tog)]
                if hh >= 2:
                    wv.append((f"Ek{hh % 2}", E_kv[hh - 2]))
                V_ev = vop(lambda e, o=o, src=src, kb=kb: e.tensor_scalar(
                    out=kvtmp[:, kb * TPC:(kb + 1) * TPC], in0=src[:, :],
                    scalar1=bqkv[:, o:o + 1], scalar2=None, op0=ALU.add), waits=wv)
                E_kv[hh] = edma(
                    kv_sl_d.ap()[0:VOFF].rearrange("(hh p t) -> hh p t", p=128, t=TPC)[hh],
                    kvtmp[:, kb * TPC:(kb + 1) * TPC], waits=[("V", V_ev)], sem=f"Ek{hh % 2}")
            else:
                hh = o - 32
                vop(lambda e, o=o, src=src, hh=hh: e.tensor_scalar(
                    out=v_sl[:, hh * TPC:(hh + 1) * TPC], in0=src[:, :],
                    scalar1=bqkv[:, o:o + 1], scalar2=None, op0=ALU.add), waits=[("T", Tog)])
        ogm[("V", og)] = pl.c("V")
        pump(2)
    V_qkv = pl.c("V")
    E_vexp = {}
    for hh in range(H):
        vb = 2 + (hh % 2)
        for j in range(NT):
            top(lambda e, hh=hh, j=j: e.transpose(
                pA[:, :].bitcast(BF16)[:, j * 128:(j + 1) * 128],
                v_sl[:, hh * TPC + j * 128: hh * TPC + (j + 1) * 128], id_bf[:, :]),
                waits=([("V", V_qkv)] if (hh == 0 and j == 0) else
                       ([("V", pl.c("V"))] if j == 0 else ())),
                inc=("T" if j == NT - 1 else None))
        Tvt = pl.c("T")
        wv = [("T", Tvt)]
        if hh >= 2:
            wv.append((f"Ek{hh % 2}", E_vexp[hh - 2]))
        V_vt = vop(lambda e, vb=vb: e.tensor_copy(out=kvtmp[:, vb * TPC:(vb + 1) * TPC],
                                                  in_=pA[:, :].bitcast(BF16)[:, 0:TPC]),
                   waits=wv)
        E_vexp[hh] = edma(
            kv_sl_d.ap()[VOFF:].rearrange("(j p qq) -> p j qq", p=128, qq=D)[:, :, hh * 128:(hh + 1) * 128],
            kvtmp[:, vb * TPC:(vb + 1) * TPC].rearrange("p (j qq) -> p j qq", qq=128),
            waits=[("V", V_vt)], sem=f"Ek{hh % 2}")
        pump(2)
    C_kv = coll("AllGather", ALU.bypass, kv_sl_d.ap(), kv_ag_d.ap(),
                waits=[("Ek0", pl.c("Ek0")), ("Ek1", pl.c("Ek1"))], sem="Ck")
    if "q" in dbg:
        edma(dbg_outs["q"][:, :], q_fm[:, :], waits=[("V", V_qkv)])
    if "kv" in dbg:
        gdma(dbg_outs["kv"][:], kv_sl_d.ap()[:], waits=[("Ck", C_kv)])
    pump(300)
    if not want("attn"):
        return finish()
    C_wq = gen_state.get("C_wq", 0)
    V_wq = gen_state.get("V_wq", 0)

    kv_ag = kv_ag_d.ap()

    def kv_load(b, buf, waits):
        for s in range(2):
            t = 2 * b + s
            r, j = owner_slot(t)
            ksrc = kv_ag[r, 0:VOFF].rearrange("(hh p t) -> p hh t", p=128, t=TPC)[:, :, j * 128:(j + 1) * 128]
            pl.op("sync", lambda e, b_=buf, s=s, src=ksrc: e.dma_start(
                out=kblk[:, b_ * H * 256:(b_ + 1) * H * 256].rearrange("p (hh t) -> p hh t", t=256)[:, :, s * 128:(s + 1) * 128],
                in_=src), waits=(waits if s == 0 else ()), inc=f"Kb{buf}", amt=16)
            vsrc = kv_ag[r, VOFF:].rearrange("(j p qq) -> j p qq", p=128, qq=D)[j]
            pl.op("sync", lambda e, b_=buf, s=s, src=vsrc: e.dma_start(
                out=vblk[:, b_ * 2 * D + s * D: b_ * 2 * D + (s + 1) * D], in_=src), inc=f"Kb{buf}", amt=16)
        return pl.c(f"Kb{buf}")

    pl.wait("sync", [("Ck", C_kv), ("Cw", C_wq), ("V", V_wq)])
    L_blk = {0: None, 1: None}
    L_blk[0] = kv_load(0, 0, [])
    L_blk[1] = kv_load(1, 1, [])
    L_msk = {0: ldma(mskb[:, :].rearrange("p (u qq) -> p u qq", qq=128),
                     mask_in.rearrange("(ju i) qq -> i ju qq", i=128)[:, 0:8, :], sem="Mk")}

    S_exp = {}
    T_av_h = {}
    V_ev_h = {}
    prev_bh = None
    for b in range(NBLK):
        jmin = b // 4
        cols = (NT - jmin) * 128
        nmm = (cols + 511) // 512
        if b % 4 == 0 and b > 0:
            L_msk[jmin] = ldma(mskb[:, :].rearrange("p (u qq) -> p u qq", qq=128),
                               mask_in.rearrange("(ju i) qq -> i ju qq", i=128)[:, jmin * 8:(jmin + 1) * 8, :],
                               waits=[("V", pl.c("V"))], sem="Mk")
        for h in range(H):
            bh = b * H + h
            for s in range(2):
                ps = pp[s]
                tw = []
                if prev_bh is not None:
                    tw.append(("S", S_exp[(prev_bh, s)]))
                if h == 0 and s == 0:
                    tw.append((f"Kb{b % 2}", L_blk[b]))
                    if b == 0:
                        tw.append(("V", V_wq))
                for c0 in range(nmm):
                    w0, w1 = c0 * 512, min(cols, c0 * 512 + 512)
                    top(lambda e, b=b, h=h, s=s, ps=ps, w0=w0, w1=w1, jm=jmin: e.matmul(
                        ps[:, w0:w1],
                        kblk[:, (b % 2) * H * 256 + h * 256 + s * 128: (b % 2) * H * 256 + h * 256 + (s + 1) * 128],
                        q_fm[:, h * TPC + jm * 128 + w0: h * TPC + jm * 128 + w1],
                        start=True, stop=True),
                        waits=(tw if c0 == 0 else ()),
                        inc=("T" if c0 == nmm - 1 else None))
                T_s = pl.c("T")
                sw = [("T", T_s), ("V", V_csc)]
                if prev_bh is not None:
                    sw.append(("T", T_av_h[prev_bh]))
                S_exp[(bh, s)] = sop(lambda e, ps=ps, cols=cols, s=s: e.activation(
                    out=ptb[:, s * 1024: s * 1024 + cols], in_=ps[:, 0:cols], func=ACTF.Exp,
                    bias=csc[:, 0:1], scale=csc[:, 1:2]), waits=sw)
                u = (b - 4 * jmin) * 2 + s
                V_mk = vop(lambda e, s=s, u=u: e.tensor_tensor(
                    out=ptb[:, s * 1024: s * 1024 + 128], in0=ptb[:, s * 1024: s * 1024 + 128],
                    in1=mskb[:, u * 128:(u + 1) * 128], op=ALU.mult),
                    waits=[("S", S_exp[(bh, s)]), ("Mk", L_msk[jmin])])
                lw = [("V", V_mk)]
                if s == 0 and prev_bh is not None:
                    lw.append(("V", V_ev_h[prev_bh]))
                for c0 in range(nmm):
                    w0, w1 = c0 * 512, min(cols, c0 * 512 + 512)
                    top(lambda e, s=s, w0=w0, w1=w1: e.matmul(
                        pD[0:1, w0:w1], ones_bf[:, 0:1], ptb[:, s * 1024 + w0: s * 1024 + w1],
                        start=(s == 0), stop=(s == 1)),
                        waits=(lw if c0 == 0 else ()))
                for c0 in range(nmm):
                    w0, w1 = c0 * 512, min(cols, c0 * 512 + 512)
                    top(lambda e, b=b, s=s, h=h, w0=w0, w1=w1: e.matmul(
                        pC[:, w0:w1],
                        vblk[:, (b % 2) * 2 * D + s * D + h * 128: (b % 2) * 2 * D + s * D + (h + 1) * 128],
                        ptb[:, s * 1024 + w0: s * 1024 + w1],
                        start=(s == 0), stop=(s == 1)),
                        inc=("T" if (s == 1 and c0 == nmm - 1) else None))
            T_av_h[bh] = pl.c("T")
            T_av = pl.c("T")
            col0 = jmin * 128
            if b == 0:
                vop(lambda e, h=h, cols=cols, col0=col0: e.tensor_copy(
                    out=lsl[:, h * TPC + col0: h * TPC + col0 + cols], in_=pD[0:1, 0:cols]),
                    waits=[("T", T_av)], inc=None)
                V_ev_h[bh] = vop(lambda e, h=h, cols=cols, col0=col0: e.tensor_copy(
                    out=att[:, h * TPC + col0: h * TPC + col0 + cols], in_=pC[:, 0:cols]))
            else:
                vop(lambda e, h=h, cols=cols, col0=col0: e.tensor_tensor(
                    out=lsl[:, h * TPC + col0: h * TPC + col0 + cols],
                    in0=lsl[:, h * TPC + col0: h * TPC + col0 + cols],
                    in1=pD[0:1, 0:cols], op=ALU.add), waits=[("T", T_av)], inc=None)
                V_ev_h[bh] = vop(lambda e, h=h, cols=cols, col0=col0: e.tensor_tensor(
                    out=att[:, h * TPC + col0: h * TPC + col0 + cols],
                    in0=att[:, h * TPC + col0: h * TPC + col0 + cols],
                    in1=pC[:, 0:cols], op=ALU.add))
            prev_bh = bh
        if b + 2 < NBLK:
            L_blk[b + 2] = kv_load(b + 2, b % 2, [("T", pl.c("T"))])
    V_attn = pl.c("V")
    if "att" in dbg:
        edma(dbg_outs["att"][:, :], att[:, :], waits=[("V", V_attn)])
    if "lsl" in dbg:
        edma(dbg_outs["lsl"][:, :], lsl[:, :], waits=[("V", V_attn)])
    if not want("wout"):
        return finish()

    V_rec = vop(lambda e: e.reciprocal(out=lsl[:, :], in_=lsl[:, :]), waits=[("V", V_attn)])
    L_xr = 0
    for k in range(KD):
        L_xr = ldma(x_rl[:, k * TPC:(k + 1) * TPC], x_in[:, k * 128:(k + 1) * 128], transpose=True,
                    waits=([("V", V_attn)] if k == 0 else ()), sem="Lx")
    V_nrm = V_rec
    for h in range(H):
        for hf in range(2):
            top(lambda e, h=h, hf=hf: e.matmul(
                pD[:, hf * 512:(hf + 1) * 512], onesf_col[0:1, 0:128],
                lsl[0:1, h * TPC + hf * 512: h * TPC + hf * 512 + 512], start=True, stop=True),
                waits=([("V", V_rec), ("V", V_nrm)] if hf == 0 else ()),
                inc=("T" if hf == 1 else None))
        T_n = pl.c("T")
        V_nrm = vop(lambda e, h=h: e.tensor_tensor(
            out=att_bf[:, h * TPC:(h + 1) * TPC], in0=att[:, h * TPC:(h + 1) * TPC],
            in1=pD[:, :], op=ALU.mult), waits=[("T", T_n)])
    V_nrmf = pl.c("V")
    wom = {}
    for og in range(8):
        buf = og % 2
        w = [("T", wom[("T", og - 2)])] if og >= 2 else []
        Lw = ldma(woutbuf[:, buf * 4096:(buf + 1) * 4096].rearrange("p (k o) -> p k o", o=256),
                  wout_in[:, og * 256:(og + 1) * 256].rearrange("(k p) o -> p k o", p=128),
                  waits=w, sem=f"Wq{buf}")
        ps_pair = (pA, pB) if og % 2 == 0 else (pC, pD)
        tw = [(f"Wq{buf}", Lw), ("V", V_nrmf)]
        if og >= 2:
            tw.append(("V", wom[("V", og - 2)]))
        first = True
        for k in range(KD):
            for oo in range(2):
                tgt = ps_pair[oo]
                for hf in range(2):
                    top(lambda e, b=buf, k=k, oo=oo, hf=hf, tgt=tgt: e.matmul(
                        tgt[:, hf * 512:(hf + 1) * 512],
                        woutbuf[:, b * 4096 + k * 256 + oo * 128: b * 4096 + k * 256 + oo * 128 + 128],
                        att_bf[:, k * TPC + hf * 512: k * TPC + hf * 512 + 512],
                        start=(k == 0), stop=(k == KD - 1)),
                        waits=(tw if first else ()),
                        inc=("T" if (k == KD - 1 and oo == 1 and hf == 1) else None))
                    first = False
        wom[("T", og)] = pl.c("T")
        Tog = pl.c("T")
        for oo in range(2):
            o = og * 2 + oo
            src = ps_pair[oo]
            vop(lambda e, o=o, src=src: e.tensor_scalar(
                out=csc2[:, :], in0=src[:, :],
                scalar1=boutv[:, o:o + 1], scalar2=dv[:, 32 + o:33 + o],
                op0=ALU.add, op1=ALU.mult), waits=[("T", Tog), ("Lx", L_xr)], inc=None)
            vop(lambda e, o=o: e.tensor_tensor(
                out=x1[:, o * TPC:(o + 1) * TPC], in0=csc2[:, :],
                in1=x_rl[:, o * TPC:(o + 1) * TPC], op=ALU.add))
        wom[("V", og)] = pl.c("V")
    V_x1 = pl.c("V")
    if "x1" in dbg:
        edma(dbg_outs["x1"][:, :], x1[:, :], waits=[("V", V_x1)])
    if not want("ln2"):
        return finish()

    for k in range(KD):
        V_c = vop(lambda e, k=k: e.tensor_copy(out=x1b[:, :], in_=x1[:, k * TPC:(k + 1) * TPC]),
                  waits=([("V", V_x1)] if k == 0 else [("T", pl.c("T"))]))
        S_sq = sop(lambda e: e.activation(out=sq2[:, :], in_=x1b[:, :], func=ACTF.Square),
                   waits=[("V", V_c)])
        for hf in range(2):
            top(lambda e, k=k, hf=hf: e.matmul(
                pB[0:1, hf * 512:(hf + 1) * 512], ones_bf[:, 0:1],
                x1b[:, hf * 512: hf * 512 + 512],
                start=(k == 0), stop=(k == KD - 1)),
                waits=([("V", V_c)] if hf == 0 else ()))
            top(lambda e, k=k, hf=hf: e.matmul(
                pC[0:1, hf * 512:(hf + 1) * 512], ones_bf[:, 0:1],
                sq2[:, hf * 512: hf * 512 + 512],
                start=(k == 0), stop=(k == KD - 1)),
                waits=([("S", S_sq)] if hf == 0 else ()),
                inc=("T" if hf == 1 else None))
    T_ln2 = pl.c("T")
    vop(lambda e: e.tensor_scalar(out=st1[:, :], in0=pB[0:1, :], scalar1=1.0 / D, scalar2=None, op0=ALU.mult),
        waits=[("T", T_ln2)], inc=None)
    vop(lambda e: e.tensor_scalar(out=st2[:, :], in0=pC[0:1, :], scalar1=1.0 / D, scalar2=None, op0=ALU.mult), inc=None)
    vop(lambda e: e.tensor_tensor(out=st3[:, :], in0=st1[:, :], in1=st1[:, :], op=ALU.mult), inc=None)
    vop(lambda e: e.tensor_tensor(out=st2[:, :], in0=st2[:, :], in1=st3[:, :], op=ALU.subtract), inc=None)
    V_var2 = vop(lambda e: e.tensor_scalar(out=st2[:, :], in0=st2[:, :], scalar1=1e-5, scalar2=None, op0=ALU.add))
    S_rs2 = sop(lambda e: e.activation(out=st2[:, :], in_=st2[:, :], func=ACTF.Sqrt), waits=[("V", V_var2)])
    vop(lambda e: e.reciprocal(out=st2[:, :], in_=st2[:, :]), waits=[("S", S_rs2)], inc=None)
    V_mr2 = vop(lambda e: e.tensor_tensor(out=st1[:, :], in0=st1[:, :], in1=st2[:, :], op=ALU.mult))
    for (ix, (dstb, srcv)) in enumerate(((abc, st2), (bbc, st1))):
        for hf in range(2):
            top(lambda e, s=srcv, hf=hf: e.matmul(
                pD[:, hf * 512:(hf + 1) * 512], onesf_col[0:1, 0:128],
                s[0:1, hf * 512: hf * 512 + 512], start=True, stop=True),
                waits=([("V", max(V_mr2, pl.c("V")))] if hf == 0 else ()),
                inc=("T" if hf == 1 else None))
        Tb = pl.c("T")
        vop(lambda e, d=dstb: e.tensor_copy(out=d[:, :], in_=pD[:, :]), waits=[("T", Tb)])
    V_bc2 = pl.c("V")
    V_mm = V_bc2
    for k in range(KD):
        vop(lambda e, k=k: e.tensor_tensor(out=dsc[:, :], in0=x1[:, k * TPC:(k + 1) * TPC], in1=abc[:, :],
                                           op=ALU.mult), inc=None)
        vop(lambda e: e.tensor_tensor(out=dsc[:, :], in0=dsc[:, :], in1=bbc[:, :], op=ALU.subtract), inc=None)
        vop(lambda e, k=k: e.tensor_scalar(out=h2sl[:, k * TPC:(k + 1) * TPC], in0=dsc[:, :],
                                           scalar1=dv[:, 16 + k:17 + k], scalar2=modv[:, 48 + k:49 + k],
                                           op0=ALU.mult, op1=ALU.add), inc=None)
        if k == 0:
            vop(lambda e: e.tensor_copy(out=mmax[:, :], in_=h2sl[:, 0:TPC]), inc=None)
            V_mm = vop(lambda e: e.tensor_copy(out=mmin[:, :], in_=h2sl[:, 0:TPC]))
        else:
            vop(lambda e, k=k: e.tensor_tensor(out=mmax[:, :], in0=mmax[:, :],
                                               in1=h2sl[:, k * TPC:(k + 1) * TPC], op=ALU.max), inc=None)
            V_mm = vop(lambda e, k=k: e.tensor_tensor(out=mmin[:, :], in0=mmin[:, :],
                                                      in1=h2sl[:, k * TPC:(k + 1) * TPC], op=ALU.min))
    V_am = vop(lambda e: e.scalar_tensor_tensor(out=mmax[:, :], in0=mmin[:, :], scalar=-1.0, in1=mmax[:, :],
                                                op0=ALU.mult, op1=ALU.max), waits=[("V", V_mm)])

    def token_scales(src_mm, V_ready, dst_s2bc, dst_ub, ub_scale_col):
        for i in range(NT):
            top(lambda e, i=i, s=src_mm: e.transpose(pA[:, i * 128:(i + 1) * 128],
                                                     s[:, i * 128:(i + 1) * 128], id_f[:, :]),
                waits=([("V", V_ready)] if i == 0 else ()), inc=("T" if i == NT - 1 else None))
        T_amT = pl.c("T")
        for i in range(NT):
            vop_s(lambda e, i=i: e.reduce_max(out=stok[:, i:i + 1], in_=pA[:, i * 128:(i + 1) * 128], axis=AXX),
                  waits=([("T", T_amT)] if i == 0 else ()))
        vop_s(lambda e: e.tensor_scalar(out=stok[:, 0:8], in0=stok[:, 0:8], scalar1=1e-5, scalar2=None,
                                        op0=ALU.max))
        vop_s(lambda e: e.tensor_scalar(out=stok[:, 8:16], in0=stok[:, 0:8], scalar1=1.0 / 127.0, scalar2=None,
                                        op0=ALU.mult))
        vop_s(lambda e: e.reciprocal(out=stok[:, 0:8], in_=stok[:, 0:8]))
        V_s2 = vop_s(lambda e: e.tensor_scalar(out=stok[:, 0:8], in0=stok[:, 0:8], scalar1=127.0, scalar2=None,
                                               op0=ALU.mult))
        T_st = top(lambda e: e.transpose(pB[0:16, 0:128], stok[:, 0:16], id_f[:, :]),
                   waits=[("V", V_s2)], inc="T")
        V_st = vop(lambda e: e.tensor_copy(out=stokT[:, :], in_=pB[0:16, 0:128]), waits=[("T", T_st)])
        ldma(s2rowA[0:1, :].rearrange("o (i c) -> o i c", c=128), stokT[0:8, :],
             waits=[("V", V_st), ("T", pl.c("T"))] + [(f"Fo{i}", pl.c(f"Fo{i}")) for i in range(4)],
             sem="Lc")
        L_s2r = ldma(s2rowB[0:1, :].rearrange("o (i c) -> o i c", c=128), stokT[8:16, :], sem="Lc")
        for i in range(NT):
            top(lambda e, i=i: e.matmul(pC[:, i * 128:(i + 1) * 128], onesf_col[0:1, 0:128],
                                        s2rowA[0:1, i * 128:(i + 1) * 128], start=True, stop=True),
                waits=([("Lc", L_s2r)] if i == 0 else ()), inc=None)
            top(lambda e, i=i: e.matmul(pD[:, i * 128:(i + 1) * 128], onesf_col[0:1, 0:128],
                                        s2rowB[0:1, i * 128:(i + 1) * 128], start=True, stop=True),
                inc=("T" if i == NT - 1 else None))
        T_bc = pl.c("T")
        vop(lambda e, d=dst_s2bc: e.tensor_copy(out=d[:, :], in_=pC[:, 0:TPC]), waits=[("T", T_bc)], inc=None)
        marks = []
        for (dub, cix) in dst_ub:
            marks.append(vop(lambda e, d=dub, cx=cix: e.tensor_scalar(
                out=d[:, :], in0=pD[:, 0:TPC], scalar1=sclc[:, cx:cx + 1], scalar2=None, op0=ALU.mult)))
        return marks[-1], pl.c("V")

    V_ub, _ = token_scales(mmax, V_am, s2bc, ((ubg, 4), (ubu, 5)), None)
    if "s2" in dbg:
        edma(dbg_outs["s2"][:, :], stok[:, :], waits=[("V", V_ub)])
    V_xq = V_ub
    for k in range(KD):
        vop(lambda e, k=k: e.tensor_tensor(out=dsc[:, :], in0=h2sl[:, k * TPC:(k + 1) * TPC], in1=s2bc[:, :],
                                           op=ALU.mult), waits=([("V", V_ub)] if k == 0 else ()), inc=None)
        vop(lambda e: e.tensor_scalar(out=dsc[:, :], in0=dsc[:, :], scalar1=MAGIC, scalar2=-MAGIC,
                                      op0=ALU.add, op1=ALU.add), inc=None)
        V_xq = vop(lambda e, k=k: e.tensor_scalar(out=xq[:, k * TPC:(k + 1) * TPC], in0=dsc[:, :],
                                                  scalar1=127.0, scalar2=-128.0, op0=ALU.min, op1=ALU.max))
    if "xq" in dbg:
        edma(dbg_outs["xq"][:, :], xq[:, :], waits=[("V", V_xq)])
    if not want("ffn1"):
        return finish()

    wagp = wq_ag_d.ap()
    fvm = {}
    F_mid = {}
    fc = 0
    for r in range(NC):
        for fs in range(2):
            buf = (r * 2 + fs) % 2
            si = r * 2 + fs
            w = [("V", fvm[4 * (si - 2) + 3])] if si >= 2 else [("Cw", C_wq), ("V", V_xq)]
            fdma(wslab[:, buf * 2 * 16 * 512: buf * 2 * 16 * 512 + 16 * 512],
                 wagp[r, 0:D * TPC].rearrange("(dd p f) -> p dd f", p=128, f=TPC)[:, :, fs * 512:(fs + 1) * 512],
                 waits=w, sem=f"Fw{buf}")
            F_sl = fdma(wslab[:, buf * 2 * 16 * 512 + 16 * 512: (buf + 1) * 2 * 16 * 512],
                        wagp[r, D * TPC:2 * D * TPC].rearrange("(dd p f) -> p dd f", p=128, f=TPC)[:, :, fs * 512:(fs + 1) * 512],
                        sem=f"Fw{buf}")
            for fl in range(4):
                pg = pA if fl % 2 == 0 else pC
                pu = pB if fl % 2 == 0 else pD
                tw = [(f"Fw{buf}", F_sl), ("V", V_xq)]
                if fc >= 2:
                    tw.append(("V", fvm[fc - 2]))
                first = True
                gb0 = buf * 2 * 16 * 512
                ub0 = gb0 + 16 * 512
                for k in range(KD):
                    for (tgt, base) in ((pg, gb0), (pu, ub0)):
                        for hf in range(2):
                            top(lambda e, tgt=tgt, base=base, k=k, fl=fl, hf=hf: e.matmul(
                                tgt[:, hf * 512:(hf + 1) * 512],
                                wslab[:, base + k * 512 + fl * 128: base + k * 512 + (fl + 1) * 128],
                                xq[:, k * TPC + hf * 512: k * TPC + hf * 512 + 512],
                                start=(k == 0), stop=(k == KD - 1)),
                                waits=(tw if first else ()),
                                inc=("T" if (k == KD - 1 and tgt is pu and hf == 1) else None))
                            first = False
                T_fl = pl.c("T")
                mb = fc % 4
                mw = [("T", T_fl)]
                if fc >= 4:
                    mw.append((f"Fo{mb}", F_mid[fc - 4]))
                V_g1 = vop(lambda e, pg=pg: e.tensor_tensor(out=gsc1[:, :], in0=pg[:, :], in1=ubg[:, :], op=ALU.mult),
                           waits=mw)
                S_sl = sop(lambda e: e.activation(out=gsc2[:, :], in_=gsc1[:, :], func=ACTF.Silu),
                           waits=[("V", V_g1)])
                vop(lambda e, pu=pu: e.tensor_tensor(out=gsc1[:, :], in0=pu[:, :], in1=ubu[:, :], op=ALU.mult),
                    waits=[("S", S_sl)], inc=None)
                vop(lambda e, mb=mb: e.tensor_tensor(out=midb[:, mb * TPC:(mb + 1) * TPC], in0=gsc2[:, :],
                                                     in1=gsc1[:, :], op=ALU.mult), inc=None)
                if fc == 0:
                    vop(lambda e: e.tensor_copy(out=fmx[:, :], in_=midb[:, 0:TPC]), inc=None)
                    fvm[fc] = vop(lambda e: e.tensor_copy(out=fmn[:, :], in_=midb[:, 0:TPC]))
                else:
                    vop(lambda e, mb=mb: e.tensor_tensor(out=fmx[:, :], in0=fmx[:, :],
                                                         in1=midb[:, mb * TPC:(mb + 1) * TPC], op=ALU.max), inc=None)
                    fvm[fc] = vop(lambda e, mb=mb: e.tensor_tensor(out=fmn[:, :], in0=fmn[:, :],
                                                                   in1=midb[:, mb * TPC:(mb + 1) * TPC], op=ALU.min))
                F_mid[fc] = fdma(mid_d[fc], midb[:, mb * TPC:(mb + 1) * TPC], waits=[("V", fvm[fc])], sem=f"Fo{mb}")
                fc += 1
    V_f1 = pl.c("V")
    if "mid" in dbg:
        for u in range(KF):
            gdma(dbg_outs["mid"][u * 128:(u + 1) * 128, :], mid_d[u],
                 waits=[(f"Fo{i}", pl.c(f"Fo{i}")) for i in range(4)])
    if not want("out"):
        return finish()

    V_am2 = vop(lambda e: e.scalar_tensor_tensor(out=fmx[:, :], in0=fmn[:, :], scalar=-1.0, in1=fmx[:, :],
                                                 op0=ALU.mult, op1=ALU.max), waits=[("V", V_f1)])
    V_ubd, _ = token_scales(fmx, V_am2, s2bc, ((ubd, 6),), None)

    F_mq = {}
    pvm = {}
    import os as _os
    NPASS = int(_os.environ.get("KNPASS", "4"))
    for q in range(NPASS):
        F_wd = {}
        F_wd[0] = fdma(wdslab[:, 0:8 * 512],
                       wagp[0, 2 * D * TPC:].rearrange("(ff p m) -> p ff m", p=128, m=D)[:, :, q * 512:(q + 1) * 512],
                       waits=[("T", pl.c("T")), ("V", pl.c("V"))], sem="Fw0")
        for fci in range(KF):
            r = fci // 8
            ffl = fci % 8
            mb = fci % 4
            qb = fci % 2
            if ffl == 0 and r + 1 < NC:
                F_wd[r + 1] = fdma(wdslab[:, ((r + 1) % 2) * 8 * 512:(((r + 1) % 2) + 1) * 8 * 512],
                                   wagp[r + 1, 2 * D * TPC:].rearrange("(ff p m) -> p ff m", p=128, m=D)[:, :, q * 512:(q + 1) * 512],
                                   waits=[("T", pl.c("T"))], sem=f"Fw{(r + 1) % 2}")
            if q == 0:
                wv = []
                if fci == 0:
                    wv += [(f"Fo{i}", pl.c(f"Fo{i}")) for i in range(4)]
                if fci >= 2:
                    wv.append(("V", fvm.get(("q0", fci - 2), 0)))
                Fm = fdma(midb[:, mb * TPC:(mb + 1) * TPC], mid_d[fci], waits=wv, sem=f"Fm{fci % 2}")
                vop(lambda e, mb=mb: e.tensor_tensor(out=fsc1[:, :], in0=midb[:, mb * TPC:(mb + 1) * TPC],
                                                     in1=s2bc[:, :], op=ALU.mult),
                    waits=[(f"Fm{fci % 2}", Fm), ("V", V_ubd)], inc=None)
                vop(lambda e: e.tensor_scalar(out=fsc1[:, :], in0=fsc1[:, :], scalar1=MAGIC, scalar2=-MAGIC,
                                              op0=ALU.add, op1=ALU.add), inc=None)
                wv2 = [("T", pl.c("T"))] if fci >= 2 else []
                if fci >= 2:
                    wv2.append((f"Fq{fci % 2}", F_mq[fci - 2]))
                V_q = vop(lambda e, qb=qb: e.tensor_scalar(out=midqb[:, qb * TPC:(qb + 1) * TPC], in0=fsc1[:, :],
                                                           scalar1=127.0, scalar2=-128.0, op0=ALU.min, op1=ALU.max),
                          waits=wv2)
                fvm[("q0", fci)] = V_q
                F_mq[fci] = fdma(midq_d[fci], midqb[:, qb * TPC:(qb + 1) * TPC], waits=[("V", V_q)], sem=f"Fq{fci % 2}")
                F_rdy = ("V", V_q)
            else:
                wv = [("T", pl.c("T"))] if fci >= 2 else []
                if fci == 0:
                    wv.append(("Fq0", pl.c("Fq0")))
                    wv.append(("Fq1", pl.c("Fq1")))
                F_rdy = (f"Fm{fci % 2}",
                         fdma(midqb[:, qb * TPC:(qb + 1) * TPC], midq_d[fci], waits=wv, sem=f"Fm{fci % 2}"))
            first = True
            for ol in range(4):
                tgt = pp[ol]
                for hf in range(2):
                    tw = []
                    if first:
                        tw = [F_rdy, (f"Fw{r % 2}", F_wd[r])]
                        if q == 0:
                            tw.append(("V", fvm[("q0", fci)]))
                        if fci == 0:
                            tw.append(("V", pvm.get(("Vq", q - 1), 0)))
                    top(lambda e, r=r, ffl=ffl, ol=ol, hf=hf, tgt=tgt, qb=qb, fci=fci: e.matmul(
                        tgt[:, hf * 512:(hf + 1) * 512],
                        wdslab[:, (r % 2) * 8 * 512 + ffl * 512 + ol * 128: (r % 2) * 8 * 512 + ffl * 512 + (ol + 1) * 128],
                        midqb[:, qb * TPC + hf * 512: qb * TPC + hf * 512 + 512],
                        start=(fci == 0), stop=(fci == KF - 1)),
                        waits=tw,
                        inc=("T" if (ol == 3 and hf == 1) else None))
                    first = False
        T_q = pl.c("T")
        pvm[("Tq", q)] = T_q
        for ol in range(4):
            o = q * 4 + ol
            vop(lambda e, ol=ol: e.tensor_tensor(out=fsc1[:, :], in0=pp[ol][:, :], in1=ubd[:, :], op=ALU.mult),
                waits=[("T", T_q)], inc=None)
            V_x2 = vop(lambda e, o=o: e.scalar_tensor_tensor(
                out=fsc2[:, :], in0=fsc1[:, :], scalar=dv[:, 48 + o:49 + o],
                in1=x1[:, o * TPC:(o + 1) * TPC], op0=ALU.mult, op1=ALU.add))
            for i in range(NT):
                top(lambda e, ol=ol, i=i: e.transpose(pp[ol][:, i * 128:(i + 1) * 128],
                                                      fsc2[:, i * 128:(i + 1) * 128], id_f[:, :]),
                    waits=([("V", V_x2)] if i == 0 else ()), inc=("T" if i == NT - 1 else None))
            T_tr = pl.c("T")
            vop(lambda e, ol=ol, o=o: e.tensor_copy(
                out=x2tm[:, :].rearrange("p (i dcol) -> p i dcol", dcol=D)[:, :, o * 128:(o + 1) * 128],
                in_=pp[ol][:, :].rearrange("p (i qq) -> p i qq", qq=128)), waits=[("T", T_tr)])
        pvm[("Vq", q)] = pl.c("V")
    V_x2f = pl.c("V")
    for i in range(NT):
        edma(out_d[i * 128:(i + 1) * 128, :], x2tm[:, i * D:(i + 1) * D],
             waits=([("V", V_x2f)] if i == 0 else ()), sem="Oo")
    return finish()


def _emit(nc, pl, perm):
    from contextlib import ExitStack
    names = sorted(set(pl.cnt.keys()) | {"T", "V", "S"})
    with ExitStack() as st_, nc.Block() as block:
        sems = {n: st_.enter_context(nc.semaphore("sem_" + n)) for n in names}

        @block.sync
        def _(e):
            for fn in pl.ops["sync"]:
                fn(e, sems)

        @block.tensor
        def _(e):
            for fn in pl.ops["tensor"]:
                fn(e, sems)

        @block.vector
        def _(e):
            for fn in pl.ops["vector"]:
                fn(e, sems)

        @block.scalar
        def _(e):
            for fn in pl.ops["scalar"]:
                fn(e, sems)

        @block.gpsimd
        def _(e):
            for fn in pl.ops["gpsimd"]:
                fn(e, sems)

    perm.close()
    return nc


def make_consts():
    bf = ml_dtypes.bfloat16
    return {
        "ones_bf": np.ones((128, 128), bf),
        "id_bf": np.eye(128, dtype=np.float32).astype(bf),
        "id_f": np.eye(128, dtype=np.float32),
    }


def make_masks(c):
    bf = ml_dtypes.bfloat16
    tiles = tiles_of_core(c)
    m = np.zeros((8, 8, 128, 128), np.float32)
    tri = np.tril(np.ones((128, 128), np.float32)).T
    for j in range(8):
        g = tiles[j]
        for u in range(8):
            b = 4 * j + u // 2
            s = u % 2
            t = 2 * b + s
            if t < g:
                m[j, u] = 1.0
            elif t == g:
                m[j, u] = tri
    return m.reshape(8 * 8 * 128, 128).astype(bf)


def host_inputs(inputs):
    x = np.asarray(inputs["L0_attn_x"]).reshape(S, D)
    wqkv = np.ascontiguousarray(np.asarray(inputs["L0_attn_W_qkv"]))
    bq = np.ascontiguousarray(np.asarray(inputs["L0_attn_b_qkv"]))
    wo = np.ascontiguousarray(np.asarray(inputs["L0_attn_W_out"]))
    bo = np.ascontiguousarray(np.asarray(inputs["L0_attn_b_out"]))
    wg = np.asarray(inputs["L0_ffn_w_gate"], np.float32)
    wu = np.asarray(inputs["L0_ffn_w_up"], np.float32)
    wd = np.asarray(inputs["L0_ffn_w_down"], np.float32)
    cvec = np.ascontiguousarray(np.asarray(inputs["L0_c"], np.float32).reshape(D))
    wmod = np.asarray(inputs["L0_wmod"], np.float32)
    consts = make_consts()
    ims = []
    for c in range(NC):
        tiles = tiles_of_core(c)
        xs = np.ascontiguousarray(np.concatenate([x[g * 128:(g + 1) * 128] for g in tiles], axis=0))
        im = {
            "x": xs,
            "wqkv": wqkv, "bqkv": bq, "wout": wo, "bout": bo,
            "cvec": cvec,
            "wmods": np.ascontiguousarray(wmod[:, c * WM_SL:(c + 1) * WM_SL]),
            "wgs": np.ascontiguousarray(wg[c * TPC:(c + 1) * TPC, :]),
            "wus": np.ascontiguousarray(wu[c * TPC:(c + 1) * TPC, :]),
            "wds": np.ascontiguousarray(wd[:, c * TPC:(c + 1) * TPC]),
            "masks": make_masks(c),
        }
        im.update(consts)
        ims.append(im)
    return ims


def kernel(**inputs):
    ims = host_inputs(inputs)
    nc = build()
    res = run_bass_kernel_spmd(nc, ims, core_ids=list(range(NC)))
    out = np.empty((S, D), np.float32)
    for c in range(NC):
        o = res.results[c]["out"]
        for j, g in enumerate(tiles_of_core(c)):
            out[g * 128:(g + 1) * 128] = o[j * 128:(j + 1) * 128]
    return out.reshape(1, S, D)


# version 02
# speedup vs baseline: 1.1721x; 1.1721x over previous
from contextlib import ExitStack

import numpy as np
import ml_dtypes

import concourse.bass as bass
import concourse.mybir as mybir
from concourse.bass_utils import run_bass_kernel_spmd

F32 = mybir.dt.float32
BF16 = mybir.dt.bfloat16
AXX = mybir.AxisListType.X
ALU = mybir.AluOpType
ACTF = mybir.ActivationFunctionType

NC = 8
D = 2048
S = 8192
H = 16
DF = 8192
TPC = 1024
NT = 8
KD = 16
KF = 64
MAGIC = float(np.float32(1.5 * 2 ** 23))
INV_SQRT_HD = float(np.float32(1.0) / np.float32(np.sqrt(np.float32(128.0))))
EXP_SHIFT = -20.0
WM_SL = 12 * 128
NBLK = 32
WQP = 3 * D * TPC
KVP = H * 128 * TPC + NT * 128 * D
VOFF = H * 128 * TPC
W_CNT = float(DF * D)


def tiles_of_core(c):
    return [8 * j + (c if j % 2 == 0 else 7 - c) for j in range(NT)]


def owner_slot(t):
    j, m = divmod(t, 8)
    return (m if j % 2 == 0 else 7 - m), j


class Plan:
    def __init__(self):
        self.ops = {k: [] for k in ("sync", "tensor", "vector", "scalar", "gpsimd")}
        from collections import defaultdict
        self.cnt = defaultdict(int)

    AUTO = {"vector": "V", "scalar": "S"}

    def op(self, eng, build_fn, waits=(), inc=None, amt=1, auto=True):
        aut = self.AUTO.get(eng) if auto else None
        w = list(waits)
        if aut is not None:
            if self.cnt[aut] > 0:
                w.append((aut, self.cnt[aut]))
            if inc is None:
                inc = aut
            assert inc == aut
        w = tuple((s, v) for (s, v) in w if v and v > 0)
        if inc is not None:
            self.cnt[inc] += amt
        post = self.cnt[inc] if inc is not None else None

        def fn(e, sems, _w=w, _b=build_fn, _i=inc, _a=amt):
            for sname, v in _w:
                e.wait_ge(sems[sname], v)
            ins = _b(e)
            if _i is not None:
                ins.then_inc(sems[_i], _a)

        self.ops[eng].append(fn)
        return post

    def wait(self, eng, waits):
        self.op(eng, lambda e: None, waits=waits, auto=False)

    def c(self, sem):
        return self.cnt.get(sem, 0) if hasattr(self.cnt, "get") else self.cnt[sem]


class Arena:
    def __init__(self, nc, ctx, nbytes):
        self.nc = nc
        self.nbytes = nbytes
        self.h = ctx.enter_context(nc.sbuf_tensor("dynarena", [128, nbytes], mybir.dt.uint8))
        self.base = nc.lookup_mloc(self.h).addr
        self.uid = 0

    def at(self, name, off, shape, dt):
        per = 1
        for s in shape[1:]:
            per *= s
        per *= mybir.dt.size(dt)
        assert off % 32 == 0 and off + per <= self.nbytes, (name, off, per)
        self.uid += 1
        h = self.nc.virtual_sbuf_tensor(f"{name}_{self.uid}", shape, dt)
        self.nc._tensor(h.name, h.shape, h.dtype, type="SB")
        m = self.nc.lookup_mloc(h)
        m.allocated = True
        m.addr = self.base + off
        return h


def build(dbg=(), upto="out"):
    dbg = set(dbg)
    PHASES = ["mod", "ln1", "qkv", "attn", "wout", "ln2", "ffn1", "out"]
    upto_i = PHASES.index(upto)

    def want(ph):
        return PHASES.index(ph) <= upto_i

    nc = bass.Bass("TRN2", target_bir_lowering=False)
    pl = Plan()

    def inp(name, shape, dt):
        return nc.dram_tensor(name, shape, dt, kind="ExternalInput").ap()

    x_in = inp("x", [TPC, D], BF16)
    wqkv_in = inp("wqkv", [D, 3 * D], BF16)
    bqkv_in = inp("bqkv", [3 * D], BF16)
    wout_in = inp("wout", [D, D], BF16)
    bout_in = inp("bout", [D], BF16)
    cvec_in = inp("cvec", [D], F32)
    wmod_in = inp("wmods", [D, WM_SL], F32)
    wg_in = inp("wgs", [TPC, D], F32)
    wu_in = inp("wus", [TPC, D], F32)
    wd_in = inp("wds", [D, TPC], F32)
    mask_in = inp("masks", [8 * 8 * 128, 128], BF16)
    ones_bf_in = inp("ones_bf", [128, 128], BF16)
    id_bf_in = inp("id_bf", [128, 128], BF16)
    id_f_in = inp("id_f", [128, 128], F32)

    out_d = nc.dram_tensor("out", [TPC, D], F32, kind="ExternalOutput").ap()

    dbg_outs = {}

    def dbg_def(name, shape, dt=F32):
        if name in dbg:
            dbg_outs[name] = nc.dram_tensor("dbg_" + name, shape, dt, kind="ExternalOutput").ap()

    dbg_def("mod", [128, 96])
    dbg_def("cv", [128, 16])
    dbg_def("sil", [128, 16])
    dbg_def("msl", [128, 12])
    dbg_def("dv", [128, 64])
    dbg_def("h", [128, KD * TPC], BF16)
    dbg_def("q", [128, H * TPC], BF16)
    dbg_def("kv", [KVP], BF16)
    dbg_def("wqa", [NC * WQP], BF16)
    dbg_def("scl", [1, 8])
    dbg_def("att", [128, H * TPC])
    dbg_def("lsl", [1, H * TPC])
    dbg_def("x1", [128, KD * TPC])
    dbg_def("xq", [128, KD * TPC], BF16)
    dbg_def("s2", [128, 16])
    dbg_def("mid", [KF * 128, TPC])

    mod_sl_d = nc.dram_tensor("mod_sl", [1, WM_SL], F32)
    mod_ag_d = nc.dram_tensor("mod_ag", [NC, WM_SL], F32, addr_space="Shared")
    scl_sl_d = nc.dram_tensor("scl_sl", [1, 4], F32)
    scl_ar_d = nc.dram_tensor("scl_ar", [1, 4], F32, addr_space="Shared")
    wq_sl_d = nc.dram_tensor("wq_sl", [WQP], BF16)
    wq_ag_d = nc.dram_tensor("wq_ag", [NC, WQP], BF16, addr_space="Shared")
    kv_sl_d = nc.dram_tensor("kv_sl", [KVP], BF16)
    kv_ag_d = nc.dram_tensor("kv_ag", [NC, KVP], BF16, addr_space="Shared")
    mid_d = nc.dram_tensor("mid", [KF, 128, TPC], F32)
    midq_d = nc.dram_tensor("midq", [KF, 128, TPC], BF16)

    perm = ExitStack()

    def sbp(name, shape, dt):
        return perm.enter_context(nc.sbuf_tensor(name + "_sb", shape, dt))

    ones_bf = sbp("ones_bf", [128, 128], BF16)
    id_bf = sbp("id_bf", [128, 128], BF16)
    id_f = sbp("id_f", [128, 128], F32)
    csc = sbp("csc", [128, 4], F32)
    cv = sbp("cv", [128, 16], F32)
    sil = sbp("sil", [128, 16], F32)
    t16a = sbp("t16a", [128, 16], F32)
    t16b = sbp("t16b", [128, 16], F32)
    modv = sbp("modv", [128, 96], F32)
    dv = sbp("dv", [128, 64], F32)
    bqkv = sbp("bqkv", [128, 48], F32)
    boutv = sbp("boutv", [128, 16], F32)
    sclv = sbp("sclv", [1, 8], F32)
    sclc = sbp("sclc", [128, 8], F32)
    accv = sbp("accv", [128, 4], F32)
    wqacc = sbp("wqacc", [128, 4], F32)
    stok = sbp("stok", [128, 16], F32)
    stokT = sbp("stokT", [16, 128], F32)

    AR_BYTES = 206848
    ar = Arena(nc, perm, AR_BYTES)

    q_fm = ar.at("q_fm", 0, [128, H * TPC], BF16)
    x_fm = ar.at("x_fm", 32768, [128, KD * TPC], BF16)
    h_fm = ar.at("h_fm", 65536, [128, KD * TPC], BF16)
    v_sl = ar.at("v_sl", 98304, [128, H * TPC], BF16)
    wbuf = ar.at("wbuf", 131072, [128, 2 * 16 * 256], BF16)
    kvtmp = ar.at("kvtmp", 147456, [128, 4 * TPC], BF16)
    sqbuf = ar.at("sqbuf", 147456, [128, TPC], BF16)
    wfbuf = ar.at("wfbuf", 155648, [128, 2 * 2048], F32)
    wmodbuf = ar.at("wmodbuf", 155648, [128, 2 * WM_SL], F32)
    mtile = ar.at("mtile", 172032, [128, 2048], BF16)
    mttile = ar.at("mttile", 176128, [128, 2048], BF16)
    st1 = ar.at("st1", 180224, [1, TPC], F32)
    st2 = ar.at("st2", 184320, [1, TPC], F32)
    st3 = ar.at("st3", 188416, [1, TPC], F32)
    abc = ar.at("abc", 192512, [128, TPC], F32)
    bbc = ar.at("bbc", 196608, [128, TPC], F32)
    vsc1 = ar.at("vsc1", 200704, [128, TPC], F32)
    onesf_col = ar.at("onesf_col", 204800, [128, 128], F32)

    att = ar.at("att", 32768, [128, H * TPC], F32)
    lsl = ar.at("lsl", 98304, [1, H * TPC], F32)
    kblk = ar.at("kblk", 163840, [128, 2 * H * 256], BF16)
    vblk = ar.at("vblk", 180224, [128, 2 * 2 * D], BF16)
    ptb = ar.at("ptb", 196608, [128, 2 * 1024], BF16)
    mskb = ar.at("mskb", 200704, [128, 8 * 128], BF16)

    x_rl = ar.at("x_rl", 0, [128, KD * TPC], BF16)
    att_bf = ar.at("att_bf", 98304, [128, H * TPC], BF16)
    x1 = ar.at("x1", 32768, [128, KD * TPC], F32)
    woutbuf = ar.at("woutbuf", 163840, [128, 2 * 16 * 256], BF16)
    csc2 = ar.at("csc2", 180224, [128, TPC], F32)

    xq = ar.at("xq", 0, [128, KD * TPC], BF16)
    h2sl = ar.at("h2sl", 98304, [128, KD * TPC], F32)
    x1b = ar.at("x1b", 172032, [128, TPC], BF16)
    sq2 = ar.at("sq2", 174080, [128, TPC], BF16)
    mmax = ar.at("mmax", 163840, [128, TPC], F32)
    mmin = ar.at("mmin", 167936, [128, TPC], F32)
    s2bc = ar.at("s2bc", 176128, [128, TPC], F32)
    dsc = ar.at("dsc", 200704, [128, TPC], F32)

    wslab = ar.at("wslab", 98304, [128, 4 * 16 * 512], BF16)
    ubg = ar.at("ubg", 163840, [128, TPC], F32)
    ubu = ar.at("ubu", 167936, [128, TPC], F32)
    ubd = ar.at("ubd", 172032, [128, TPC], F32)
    midb = ar.at("midb", 180224, [128, 4 * TPC], F32)
    gsc1 = ar.at("gsc1", 196608, [128, TPC], F32)
    gsc2 = ar.at("gsc2", 200704, [128, TPC], F32)
    fmx = ar.at("fmx", 172032, [128, TPC], F32)
    fmn = ar.at("fmn", 176128, [128, TPC], F32)
    s2rowA = ar.at("s2rowA", 176128, [1, 1024], F32)
    s2rowB = ar.at("s2rowB", 180224, [1, 1024], F32)
    modrow = ar.at("modrow", 65536, [1, NC * WM_SL], F32)

    wdslab = ar.at("wdslab", 0, [128, 2 * 8 * 512], BF16)
    midqb = ar.at("midqb", 16384, [128, 2 * TPC], BF16)
    fsc1 = ar.at("fsc1", 20480, [128, TPC], F32)
    fsc2 = ar.at("fsc2", 24576, [128, TPC], F32)
    x2tm = ar.at("x2tm", 98304, [128, NT * D], F32)

    pp = [perm.enter_context(nc.psum_tensor(f"pp{i}", [128, 1024], F32)) for i in range(4)]
    pA, pB, pC, pD = pp

    def ldma(dst, src, waits=(), transpose=False, slow=False, sem="Lc"):
        return pl.op("sync", lambda e, d=dst, s=src, t=transpose, sl=slow: e.dma_start(
            out=d, in_=s, transpose=t, allow_slow_non_contiguous=sl),
                     waits=waits, inc=sem, amt=16)

    def edma(dst, src, waits=(), sem="Od"):
        return pl.op("sync", lambda e, d=dst, s=src: e.dma_start(out=d, in_=s),
                     waits=waits, inc=sem, amt=16)

    def fdma(dst, src, waits=(), sem="Fo"):
        return pl.op("scalar", lambda e, d=dst, s=src: e.dma_start(out=d, in_=s),
                     waits=waits, inc=sem, amt=16, auto=False)

    def gdma(dst, src, waits=(), slow=False, sem="Gd"):
        return pl.op("gpsimd", lambda e, d=dst, s=src, sl=slow: e.dma_start(
            out=d, in_=s, allow_slow_non_contiguous=sl),
                     waits=waits, inc=sem, amt=16)

    def vop(fn, waits=(), inc=None):
        return pl.op("vector", fn, waits=waits)

    def vop_s(fn, waits=()):
        return pl.op("vector", fn, waits=waits)

    def sop(fn, waits=(), inc="S"):
        return pl.op("scalar", fn, waits=waits)

    def top(fn, waits=(), inc=None):
        return pl.op("tensor", fn, waits=waits, inc=inc)

    def coll(kind, alu, ins_, outs_, waits=(), sem="Cm"):
        return pl.op("gpsimd", lambda e, k=kind, a=alu, i=ins_, o=outs_: e.collective_compute(
            k, a, replica_groups=[list(range(NC))], ins=[i], outs=[o]),
            waits=waits, inc=sem, amt=1)

    def finish():
        waits = [(k, pl.c(k)) for k in ("Oo", "Od", "Gd", "Em", "Ew0", "Ew1", "Ek0", "Ek1",
                                         "Fo0", "Fo1", "Fo2", "Fo3", "Fq0", "Fq1")]
        for eng in ("sync", "gpsimd"):
            pl.wait(eng, waits)
        return _emit(nc, pl, perm)

    ldma(ones_bf[:, :], ones_bf_in[:, :])
    ldma(id_bf[:, :], id_bf_in[:, :])
    ldma(id_f[:, :], id_f_in[:, :])
    ldma(cv[:, :], cvec_in.rearrange("(k p) -> p k", p=128), slow=True)
    gdma(bqkv[:, :], bqkv_in.rearrange("(o p) -> p o", p=128), slow=True, sem="Gb")
    G_bias = gdma(boutv[:, :], bout_in.rearrange("(o p) -> p o", p=128), slow=True, sem="Gb")
    L_consts = pl.c("Lc")

    vop(lambda e: e.memset(csc[:, 0:1], EXP_SHIFT), inc=None)
    vop(lambda e: e.memset(csc[:, 1:2], INV_SQRT_HD), inc=None)
    vop(lambda e: e.memset(onesf_col[:, :], 1.0), inc=None)
    V_csc = vop(lambda e: e.memset(csc[:, 2:3], 0.0))

    L_x = L_consts
    for k in range(KD):
        L_x = ldma(x_fm[:, k * TPC:(k + 1) * TPC], x_in[:, k * 128:(k + 1) * 128], transpose=True, sem="Lx")

    A1, A3, A5, A7, A9 = 0.25, -1.0 / 48, 1.0 / 480, -17.0 / 80640, 31.0 / 1451520
    pl.wait("vector", [("Lc", L_consts), ("Gb", G_bias)])
    vop_s(lambda e: e.tensor_tensor(out=t16a[:, :], in0=cv[:, :], in1=cv[:, :], op=ALU.mult))
    vop_s(lambda e: e.tensor_scalar(out=t16b[:, :], in0=t16a[:, :], scalar1=A9, scalar2=A7,
                                    op0=ALU.mult, op1=ALU.add))
    for coef in (A5, A3, A1):
        vop_s(lambda e: e.tensor_tensor(out=t16b[:, :], in0=t16a[:, :], in1=t16b[:, :], op=ALU.mult))
        vop_s(lambda e, cf=coef: e.tensor_scalar(out=t16b[:, :], in0=t16b[:, :], scalar1=cf, scalar2=None,
                                                 op0=ALU.add))
    vop_s(lambda e: e.tensor_tensor(out=t16b[:, :], in0=cv[:, :], in1=t16b[:, :], op=ALU.mult))
    vop_s(lambda e: e.tensor_scalar(out=t16b[:, :], in0=t16b[:, :], scalar1=0.5, scalar2=None, op0=ALU.add))
    V_sil = vop_s(lambda e: e.tensor_tensor(out=sil[:, :], in0=cv[:, :], in1=t16b[:, :], op=ALU.mult))

    for k in range(KD):
        buf = k % 2
        w = [("T", pl.c("T"))] if k >= 2 else []
        Lw = ldma(wmodbuf[:, buf * WM_SL:(buf + 1) * WM_SL], wmod_in[k * 128:(k + 1) * 128, :], waits=w,
                  sem=f"Wm{buf}")
        for jj in range(3):
            out_ap = pA[0:1, jj * 512:(jj + 1) * 512] if jj < 2 else pB[0:1, 0:512]
            top(lambda e, b=buf, jj=jj, k=k, oa=out_ap: e.matmul(
                oa, sil[:, k:k + 1],
                wmodbuf[:, b * WM_SL + jj * 512: b * WM_SL + (jj + 1) * 512],
                start=(k == 0), stop=(k == KD - 1)),
                waits=([(f"Wm{buf}", Lw), ("V", V_sil)] if jj == 0 else ()),
                inc=("T" if jj == 2 else None))
    T_mod = pl.c("T")
    vop(lambda e: e.tensor_copy(out=st1[0:1, 0:1024], in_=pA[0:1, 0:1024]), waits=[("T", T_mod)], inc=None)
    V_modev = vop(lambda e: e.tensor_copy(out=st1[0:1, 1024:1536] if False else st2[0:1, 0:512], in_=pB[0:1, 0:512]))
    E_mod = edma(mod_sl_d.ap()[0:1, 0:1024], st1[0:1, 0:1024], waits=[("V", V_modev)], sem="Em")
    E_mod = edma(mod_sl_d.ap()[0:1, 1024:1536], st2[0:1, 0:512], waits=[("V", V_modev)], sem="Em")
    C_mod = coll("AllGather", ALU.bypass, mod_sl_d.ap(), mod_ag_d.ap(), waits=[("Em", E_mod)], sem="Cm")
    L_mod = ldma(modrow[0:1, :], mod_ag_d.ap().rearrange("r w -> (r w)").unsqueeze(0)
                 if hasattr(mod_ag_d.ap(), "unsqueeze") else mod_ag_d.ap().rearrange("r w -> (r w)"),
                 waits=[("Cm", C_mod)], sem="Lc")
    for j in range(96):
        pt = pA if j < 48 else pB
        top(lambda e, j=j, pt=pt: e.transpose(pt[:, (j % 48):(j % 48) + 1], modrow[0:1, j * 128:(j + 1) * 128],
                                              id_f[0:1, 0:1]),
            waits=([("Lc", L_mod)] if j == 0 else ()), inc=("T" if j == 95 else None))
    T_modT = pl.c("T")
    vop(lambda e: e.tensor_copy(out=modv[:, 0:48], in_=pA[:, 0:48]), waits=[("T", T_modT)], inc=None)
    L_mod = vop(lambda e: e.tensor_copy(out=modv[:, 48:96], in_=pB[:, 0:48]))
    pl.wait("vector", [("V", L_mod)])
    vop_s(lambda e: e.tensor_scalar(out=dv[:, 0:16], in0=modv[:, 16:32], scalar1=1.0, scalar2=None, op0=ALU.add))
    vop_s(lambda e: e.tensor_scalar(out=dv[:, 16:32], in0=modv[:, 64:80], scalar1=1.0, scalar2=None, op0=ALU.add))
    V_dv = 0
    for (dst, src) in ((32, 32), (48, 80)):
        S_abs = sop(lambda e, s=src: e.activation(out=t16b[:, :], in_=modv[:, s:s + 16], func=ACTF.Abs),
                    waits=[("V", max(L_mod, pl.c("V")))])
        vop_s(lambda e, s=src: e.tensor_scalar(out=t16a[:, :], in0=modv[:, s:s + 16], scalar1=0.0, scalar2=2.0,
                                               op0=ALU.is_ge, op1=ALU.mult))
        vop_s(lambda e: e.tensor_scalar(out=t16a[:, :], in0=t16a[:, :], scalar1=-1.0, scalar2=None, op0=ALU.add))
        vop_s(lambda e: e.tensor_scalar(out=t16b[:, :], in0=t16b[:, :], scalar1=0.25, scalar2=None, op0=ALU.max),
              waits=[("S", S_abs)])
        V_dv = vop_s(lambda e, d=dst: e.tensor_tensor(out=dv[:, d:d + 16], in0=t16a[:, :], in1=t16b[:, :], op=ALU.mult))

    if "cv" in dbg:
        edma(dbg_outs["cv"][:, :], cv[:, :], waits=[("V", V_sil)])
    if "sil" in dbg:
        edma(dbg_outs["sil"][:, :], sil[:, :], waits=[("V", V_sil)])
    if "msl" in dbg:
        edma(dbg_outs["msl"][:, :], t16a[:, 0:12], waits=[("V", V_modev), ("Em", E_mod)])
    if "mod" in dbg:
        edma(dbg_outs["mod"][:, :], modv[:, :], waits=[("V", V_dv)])
    if "dv" in dbg:
        edma(dbg_outs["dv"][:, :], dv[:, :], waits=[("V", V_dv)])
    if not want("ln1"):
        return finish()

    for k in range(KD):
        S_sq = sop(lambda e, k=k: e.activation(out=sqbuf[:, :], in_=x_fm[:, k * TPC:(k + 1) * TPC],
                                               func=ACTF.Square),
                   waits=([("Lx", L_x)] if k == 0 else [("T", pl.c("T"))]))
        for hf in range(2):
            top(lambda e, k=k, hf=hf: e.matmul(
                pB[0:1, hf * 512:(hf + 1) * 512], ones_bf[:, 0:1],
                x_fm[:, k * TPC + hf * 512: k * TPC + hf * 512 + 512],
                start=(k == 0), stop=(k == KD - 1)),
                waits=([("Lx", L_x), ("V", L_mod)] if (k == 0 and hf == 0) else ()))
            top(lambda e, k=k, hf=hf: e.matmul(
                pC[0:1, hf * 512:(hf + 1) * 512], ones_bf[:, 0:1],
                sqbuf[:, hf * 512: hf * 512 + 512],
                start=(k == 0), stop=(k == KD - 1)),
                waits=([("S", S_sq)] if hf == 0 else ()),
                inc=("T" if hf == 1 else None))
    T_ln1 = pl.c("T")
    vop(lambda e: e.tensor_scalar(out=st1[:, :], in0=pB[0:1, :], scalar1=1.0 / D, scalar2=None, op0=ALU.mult),
        waits=[("T", T_ln1)], inc=None)
    vop(lambda e: e.tensor_scalar(out=st2[:, :], in0=pC[0:1, :], scalar1=1.0 / D, scalar2=None, op0=ALU.mult), inc=None)
    vop(lambda e: e.tensor_tensor(out=st3[:, :], in0=st1[:, :], in1=st1[:, :], op=ALU.mult), inc=None)
    vop(lambda e: e.tensor_tensor(out=st2[:, :], in0=st2[:, :], in1=st3[:, :], op=ALU.subtract), inc=None)
    V_var = vop(lambda e: e.tensor_scalar(out=st2[:, :], in0=st2[:, :], scalar1=1e-5, scalar2=None, op0=ALU.add))
    S_rst = sop(lambda e: e.activation(out=st2[:, :], in_=st2[:, :], func=ACTF.Sqrt), waits=[("V", V_var)])
    vop(lambda e: e.reciprocal(out=st2[:, :], in_=st2[:, :]), waits=[("S", S_rst)], inc=None)
    V_mr = vop(lambda e: e.tensor_tensor(out=st1[:, :], in0=st1[:, :], in1=st2[:, :], op=ALU.mult))
    for (ix, (dstb, srcv)) in enumerate(((abc, st2), (bbc, st1))):
        for hf in range(2):
            top(lambda e, s=srcv, hf=hf: e.matmul(
                pD[:, hf * 512:(hf + 1) * 512], onesf_col[0:1, 0:128],
                s[0:1, hf * 512: hf * 512 + 512], start=True, stop=True),
                waits=([("V", max(V_mr, pl.c("V"))), ("V", V_csc)] if hf == 0 else ()),
                inc=("T" if hf == 1 else None))
        Tb = pl.c("T")
        vop(lambda e, d=dstb: e.tensor_copy(out=d[:, :], in_=pD[:, :]), waits=[("T", Tb)])
    V_bc1 = pl.c("V")
    V_h = V_bc1
    for k in range(KD):
        vop(lambda e, k=k: e.tensor_tensor(out=vsc1[:, :], in0=x_fm[:, k * TPC:(k + 1) * TPC], in1=abc[:, :],
                                           op=ALU.mult), inc=None)
        vop(lambda e: e.tensor_tensor(out=vsc1[:, :], in0=vsc1[:, :], in1=bbc[:, :], op=ALU.subtract), inc=None)
        V_h = vop(lambda e, k=k: e.tensor_scalar(out=h_fm[:, k * TPC:(k + 1) * TPC], in0=vsc1[:, :],
                                                 scalar1=dv[:, k:k + 1], scalar2=modv[:, k:k + 1],
                                                 op0=ALU.mult, op1=ALU.add))
    if "h" in dbg:
        edma(dbg_outs["h"][:, :], h_fm[:, :], waits=[("V", V_h)])
    if not want("qkv"):
        return finish()

    slices = [(wg_in, 0, 8, 2048), (wu_in, 1, 8, 2048), (wd_in, 2, 16, 1024)]
    gen_state = {}

    def wq_gen():
        vop(lambda e: e.memset(accv[:, :], 0.0), inc=None)
        vmarks = {}
        idx = 0
        for (src, mi, ntile, width) in slices:
            for tt in range(ntile):
                w = [("V", vmarks[idx - 2])] if idx >= 2 else [("T", T_modT)]
                Fq = gdma(wfbuf[:, (idx % 2) * 2048:(idx % 2) * 2048 + width],
                          src[tt * 128:(tt + 1) * 128, :], waits=w, sem=f"Gw{idx % 2}")
                vop(lambda e, i=idx, wd=width: e.reduce_sum(
                    out=wqacc[:, 0:1], in_=wfbuf[:, (i % 2) * 2048:(i % 2) * 2048 + wd],
                    axis=AXX, apply_absolute_value=True), waits=[(f"Gw{idx % 2}", Fq)], inc=None)
                vmarks[idx] = vop(lambda e, mi=mi: e.tensor_tensor(
                    out=accv[:, mi:mi + 1], in0=accv[:, mi:mi + 1], in1=wqacc[:, 0:1], op=ALU.add))
                idx += 1
                yield
        T_ar = top(lambda e: e.matmul(pA[0:1, 0:4], onesf_col[:, 0:1], accv[:, 0:4], start=True, stop=True),
                   waits=[("V", pl.c("V")), ("V", V_csc)], inc="T")
        V_ar = vop(lambda e: e.tensor_copy(out=sclv[:, 0:4], in_=pA[0:1, 0:4]), waits=[("T", T_ar)])
        gen_state["V_pA"] = V_ar
        E_ar = edma(scl_sl_d[:, :], sclv[:, 0:4], waits=[("V", V_ar)], sem="Em")
        C_ar = coll("AllReduce", ALU.add, scl_sl_d.ap(), scl_ar_d.ap(), waits=[("Em", E_ar)], sem="Ca")
        L_ar = ldma(sclv[:, 0:4], scl_ar_d[:, :], waits=[("Ca", C_ar)], sem="Lc")
        yield
        vop_s(lambda e: e.tensor_scalar(out=sclv[:, 4:8], in0=sclv[:, 0:4], scalar1=1.0 / W_CNT, scalar2=1e-5,
                                        op0=ALU.mult, op1=ALU.max), waits=[("Lc", L_ar)])
        vop_s(lambda e: e.reciprocal(out=sclv[:, 0:4], in_=sclv[:, 4:8]))
        vop_s(lambda e: e.tensor_tensor(out=st3[0:1, 0:4], in0=sclv[:, 4:8], in1=sclv[:, 0:4], op=ALU.mult))
        vop_s(lambda e: e.tensor_scalar(out=st3[0:1, 0:4], in0=st3[0:1, 0:4], scalar1=-1.0, scalar2=2.0,
                                        op0=ALU.mult, op1=ALU.add))
        V_scl = vop_s(lambda e: e.tensor_tensor(out=sclv[:, 0:4], in0=sclv[:, 0:4], in1=st3[0:1, 0:4], op=ALU.mult))
        T_sb = top(lambda e: e.matmul(pA[:, 0:8], onesf_col[0:1, 0:128], sclv[0:1, 0:8], start=True, stop=True),
                   waits=[("V", V_scl)], inc="T")
        V_sclc = vop(lambda e: e.tensor_copy(out=sclc[:, 0:8], in_=pA[:, 0:8]), waits=[("T", T_sb)])
        gen_state["V_pA"] = V_sclc
        if "scl" in dbg:
            edma(dbg_outs["scl"][:, :], sclv[:, 0:8], waits=[("V", V_sclc)])
        yield
        emarks = {}
        idx = 0
        for (src, mi, ntile, width) in slices:
            for tt in range(ntile):
                w = [(f"Ew{idx % 2}", emarks[idx - 2])] if idx >= 2 else []
                Fq = gdma(wfbuf[:, (idx % 2) * 2048:(idx % 2) * 2048 + width],
                          src[tt * 128:(tt + 1) * 128, :], waits=w + [("V", V_sclc)], sem=f"Gw{idx % 2}")
                b0 = (idx % 2) * 2048
                vop(lambda e, b0=b0, wd=width, mi=mi: e.tensor_scalar(
                    out=wfbuf[:, b0:b0 + wd], in0=wfbuf[:, b0:b0 + wd],
                    scalar1=sclc[:, mi:mi + 1], scalar2=MAGIC, op0=ALU.mult, op1=ALU.add),
                    waits=[(f"Gw{idx % 2}", Fq)], inc=None)
                vop(lambda e, b0=b0, wd=width: e.tensor_scalar(
                    out=wfbuf[:, b0:b0 + wd], in0=wfbuf[:, b0:b0 + wd],
                    scalar1=-MAGIC, scalar2=None, op0=ALU.add), inc=None)
                V_m = vop(lambda e, b0=b0, wd=width: e.tensor_scalar(
                    out=mtile[:, 0:wd], in0=wfbuf[:, b0:b0 + wd],
                    scalar1=1.0, scalar2=-1.0, op0=ALU.min, op1=ALU.max))
                ntr = width // 128
                for u in range(ntr):
                    top(lambda e, u=u: e.transpose(
                        pC[:, :].bitcast(BF16)[:, u * 128:(u + 1) * 128],
                        mtile[:, u * 128:(u + 1) * 128], id_bf[:, :]),
                        waits=([("V", V_m)] if u == 0 else ()),
                        inc=("T" if u == ntr - 1 else None))
                T_tr = pl.c("T")
                wmt = [("T", T_tr)]
                if idx >= 1:
                    wmt.append((f"Ew{(idx - 1) % 2}", emarks[idx - 1]))
                V_mt = vop(lambda e, wd=width: e.tensor_copy(out=mttile[:, 0:wd],
                                                             in_=pC[:, :].bitcast(BF16)[:, 0:wd]),
                           waits=wmt)
                gen_state["V_pCD"] = V_mt
                gen_state["V_pA"] = max(gen_state.get("V_pA", 0), 0)
                if mi < 2:
                    base = mi * D * TPC
                    emarks[idx] = edma(
                        wq_sl_d.ap()[base:base + D * TPC].rearrange("(dd p f) -> p dd f", p=128, f=TPC)[:, :, tt * 128:(tt + 1) * 128],
                        mttile[:, :].rearrange("p (dd f) -> p dd f", f=128),
                        waits=[("V", V_mt)], sem=f"Ew{idx % 2}")
                else:
                    base = 2 * D * TPC
                    emarks[idx] = edma(
                        wq_sl_d.ap()[base:base + D * TPC].rearrange("(ff p m) -> p ff m", p=128, m=D)[:, :, tt * 128:(tt + 1) * 128],
                        mttile[:, 0:1024].rearrange("p (ff m) -> p ff m", m=128),
                        waits=[("V", V_mt)], sem=f"Ew{idx % 2}")
                idx += 1
                yield
        gen_state["C_wq"] = coll("AllGather", ALU.bypass, wq_sl_d.ap(), wq_ag_d.ap(),
                                 waits=[("Ew0", pl.c("Ew0")), ("Ew1", pl.c("Ew1"))], sem="Cw")
        gen_state["V_wq"] = pl.c("V")
        if "wqa" in dbg:
            gdma(dbg_outs["wqa"][:], wq_ag_d.ap().rearrange("r w -> (r w)"),
                 waits=[("Cw", gen_state["C_wq"])])
        yield

    need_wq = want("ln2") or ("wqa" in dbg) or ("scl" in dbg)
    wq_it = wq_gen() if need_wq else iter(())

    def pump(n):
        for _ in range(n):
            try:
                next(wq_it)
            except StopIteration:
                break

    ogm = {}
    E_kv = {}
    for og in range(24):
        buf = og % 2
        w = [("T", ogm[("T", og - 2)])] if og >= 2 else [("T", T_modT)]
        Lw = ldma(wbuf[:, buf * 4096:(buf + 1) * 4096].rearrange("p (k o) -> p k o", o=256),
                  wqkv_in[:, og * 256:(og + 1) * 256].rearrange("(k p) o -> p k o", p=128),
                  waits=w, sem=f"Wq{buf}")
        ps_pair = (pA, pB) if og % 2 == 0 else (pC, pD)
        tw = [(f"Wq{buf}", Lw), ("V", V_h)]
        if og >= 2:
            tw.append(("V", ogm[("V", og - 2)]))
        tw.append(("V", gen_state.get("V_pA" if og % 2 == 0 else "V_pCD", 0)))
        first = True
        for k in range(KD):
            for oo in range(2):
                tgt = ps_pair[oo]
                for hf in range(2):
                    top(lambda e, b=buf, k=k, oo=oo, hf=hf, tgt=tgt: e.matmul(
                        tgt[:, hf * 512:(hf + 1) * 512],
                        wbuf[:, b * 4096 + k * 256 + oo * 128: b * 4096 + k * 256 + oo * 128 + 128],
                        h_fm[:, k * TPC + hf * 512: k * TPC + hf * 512 + 512],
                        start=(k == 0), stop=(k == KD - 1)),
                        waits=(tw if first else ()),
                        inc=("T" if (k == KD - 1 and oo == 1 and hf == 1) else None))
                    first = False
        ogm[("T", og)] = pl.c("T")
        Tog = pl.c("T")
        for oo in range(2):
            o = og * 2 + oo
            src = ps_pair[oo]
            if o < 16:
                vop(lambda e, o=o, src=src: e.tensor_scalar(
                    out=q_fm[:, o * TPC:(o + 1) * TPC], in0=src[:, :],
                    scalar1=bqkv[:, o:o + 1], scalar2=None, op0=ALU.add),
                    waits=[("T", Tog)])
            elif o < 32:
                hh = o - 16
                kb = hh % 2
                wv = [("T", Tog)]
                if hh >= 2:
                    wv.append((f"Ek{hh % 2}", E_kv[hh - 2]))
                V_ev = vop(lambda e, o=o, src=src, kb=kb: e.tensor_scalar(
                    out=kvtmp[:, kb * TPC:(kb + 1) * TPC], in0=src[:, :],
                    scalar1=bqkv[:, o:o + 1], scalar2=None, op0=ALU.add), waits=wv)
                E_kv[hh] = edma(
                    kv_sl_d.ap()[0:VOFF].rearrange("(hh p t) -> hh p t", p=128, t=TPC)[hh],
                    kvtmp[:, kb * TPC:(kb + 1) * TPC], waits=[("V", V_ev)], sem=f"Ek{hh % 2}")
            else:
                hh = o - 32
                vop(lambda e, o=o, src=src, hh=hh: e.tensor_scalar(
                    out=v_sl[:, hh * TPC:(hh + 1) * TPC], in0=src[:, :],
                    scalar1=bqkv[:, o:o + 1], scalar2=None, op0=ALU.add), waits=[("T", Tog)])
        ogm[("V", og)] = pl.c("V")
        pump(2)
    V_qkv = pl.c("V")
    E_vexp = {}
    for hh in range(H):
        vb = 2 + (hh % 2)
        for j in range(NT):
            top(lambda e, hh=hh, j=j: e.transpose(
                pA[:, :].bitcast(BF16)[:, j * 128:(j + 1) * 128],
                v_sl[:, hh * TPC + j * 128: hh * TPC + (j + 1) * 128], id_bf[:, :]),
                waits=([("V", V_qkv)] if (hh == 0 and j == 0) else
                       ([("V", pl.c("V"))] if j == 0 else ())),
                inc=("T" if j == NT - 1 else None))
        Tvt = pl.c("T")
        wv = [("T", Tvt)]
        if hh >= 2:
            wv.append((f"Ek{hh % 2}", E_vexp[hh - 2]))
        V_vt = vop(lambda e, vb=vb: e.tensor_copy(out=kvtmp[:, vb * TPC:(vb + 1) * TPC],
                                                  in_=pA[:, :].bitcast(BF16)[:, 0:TPC]),
                   waits=wv)
        E_vexp[hh] = edma(
            kv_sl_d.ap()[VOFF:].rearrange("(j p qq) -> p j qq", p=128, qq=D)[:, :, hh * 128:(hh + 1) * 128],
            kvtmp[:, vb * TPC:(vb + 1) * TPC].rearrange("p (j qq) -> p j qq", qq=128),
            waits=[("V", V_vt)], sem=f"Ek{hh % 2}")
        pump(2)
    C_kv = coll("AllGather", ALU.bypass, kv_sl_d.ap(), kv_ag_d.ap(),
                waits=[("Ek0", pl.c("Ek0")), ("Ek1", pl.c("Ek1"))], sem="Ck")
    if "q" in dbg:
        edma(dbg_outs["q"][:, :], q_fm[:, :], waits=[("V", V_qkv)])
    if "kv" in dbg:
        gdma(dbg_outs["kv"][:], kv_sl_d.ap()[:], waits=[("Ck", C_kv)])
    pump(300)
    if not want("attn"):
        return finish()
    C_wq = gen_state.get("C_wq", 0)
    V_wq = gen_state.get("V_wq", 0)

    kv_ag = kv_ag_d.ap()

    def kv_load(b, buf, waits):
        for s in range(2):
            t = 2 * b + s
            r, j = owner_slot(t)
            ksrc = kv_ag[r, 0:VOFF].rearrange("(hh p t) -> p hh t", p=128, t=TPC)[:, :, j * 128:(j + 1) * 128]
            pl.op("sync", lambda e, b_=buf, s=s, src=ksrc: e.dma_start(
                out=kblk[:, b_ * H * 256:(b_ + 1) * H * 256].rearrange("p (hh t) -> p hh t", t=256)[:, :, s * 128:(s + 1) * 128],
                in_=src), waits=(waits if s == 0 else ()), inc=f"Kb{buf}", amt=16)
            vsrc = kv_ag[r, VOFF:].rearrange("(j p qq) -> j p qq", p=128, qq=D)[j]
            pl.op("sync", lambda e, b_=buf, s=s, src=vsrc: e.dma_start(
                out=vblk[:, b_ * 2 * D + s * D: b_ * 2 * D + (s + 1) * D], in_=src), inc=f"Kb{buf}", amt=16)
        return pl.c(f"Kb{buf}")

    pl.wait("sync", [("Ck", C_kv), ("Cw", C_wq), ("V", V_wq)])
    L_blk = {0: None, 1: None}
    L_blk[0] = kv_load(0, 0, [])
    L_blk[1] = kv_load(1, 1, [])
    L_msk = {0: ldma(mskb[:, :].rearrange("p (u qq) -> p u qq", qq=128),
                     mask_in.rearrange("(ju i) qq -> i ju qq", i=128)[:, 0:8, :], sem="Mk")}

    S_exp = {}
    T_av_h = {}
    V_ev_h = {}
    prev_bh = None
    for b in range(NBLK):
        jmin = b // 4
        cols = (NT - jmin) * 128
        nmm = (cols + 511) // 512
        if b % 4 == 0 and b > 0:
            L_msk[jmin] = ldma(mskb[:, :].rearrange("p (u qq) -> p u qq", qq=128),
                               mask_in.rearrange("(ju i) qq -> i ju qq", i=128)[:, jmin * 8:(jmin + 1) * 8, :],
                               waits=[("V", pl.c("V"))], sem="Mk")
        for h in range(H):
            bh = b * H + h
            V_mk = {}
            for s in range(2):
                ps = pp[s]
                tw = []
                if prev_bh is not None:
                    tw.append(("S", S_exp[(prev_bh, s)]))
                if h == 0 and s == 0:
                    tw.append((f"Kb{b % 2}", L_blk[b]))
                    if b == 0:
                        tw.append(("V", V_wq))
                for c0 in range(nmm):
                    w0, w1 = c0 * 512, min(cols, c0 * 512 + 512)
                    top(lambda e, b=b, h=h, s=s, ps=ps, w0=w0, w1=w1, jm=jmin: e.matmul(
                        ps[:, w0:w1],
                        kblk[:, (b % 2) * H * 256 + h * 256 + s * 128: (b % 2) * H * 256 + h * 256 + (s + 1) * 128],
                        q_fm[:, h * TPC + jm * 128 + w0: h * TPC + jm * 128 + w1],
                        start=True, stop=True),
                        waits=(tw if c0 == 0 else ()),
                        inc=("T" if c0 == nmm - 1 else None))
                T_s = pl.c("T")
                sw = [("T", T_s), ("V", V_csc)]
                if prev_bh is not None:
                    sw.append(("T", T_av_h[prev_bh]))
                S_exp[(bh, s)] = sop(lambda e, ps=ps, cols=cols, s=s: e.activation(
                    out=ptb[:, s * 1024: s * 1024 + cols], in_=ps[:, 0:cols], func=ACTF.Exp,
                    bias=csc[:, 0:1], scale=csc[:, 1:2]), waits=sw)
                u = (b - 4 * jmin) * 2 + s
                V_mk[s] = vop(lambda e, s=s, u=u: e.tensor_tensor(
                    out=ptb[:, s * 1024: s * 1024 + 128], in0=ptb[:, s * 1024: s * 1024 + 128],
                    in1=mskb[:, u * 128:(u + 1) * 128], op=ALU.mult),
                    waits=[("S", S_exp[(bh, s)]), ("Mk", L_msk[jmin])])
            for s in range(2):
                lw = [("V", V_mk[s])]
                if s == 0 and prev_bh is not None:
                    lw.append(("V", V_ev_h[prev_bh]))
                for c0 in range(nmm):
                    w0, w1 = c0 * 512, min(cols, c0 * 512 + 512)
                    top(lambda e, s=s, w0=w0, w1=w1: e.matmul(
                        pD[0:1, w0:w1], ones_bf[:, 0:1], ptb[:, s * 1024 + w0: s * 1024 + w1],
                        start=(s == 0), stop=(s == 1)),
                        waits=(lw if c0 == 0 else ()))
                for c0 in range(nmm):
                    w0, w1 = c0 * 512, min(cols, c0 * 512 + 512)
                    top(lambda e, b=b, s=s, h=h, w0=w0, w1=w1: e.matmul(
                        pC[:, w0:w1],
                        vblk[:, (b % 2) * 2 * D + s * D + h * 128: (b % 2) * 2 * D + s * D + (h + 1) * 128],
                        ptb[:, s * 1024 + w0: s * 1024 + w1],
                        start=(s == 0), stop=(s == 1)),
                        inc=("T" if (s == 1 and c0 == nmm - 1) else None))
            T_av_h[bh] = pl.c("T")
            T_av = pl.c("T")
            col0 = jmin * 128
            if b == 0:
                vop(lambda e, h=h, cols=cols, col0=col0: e.tensor_copy(
                    out=lsl[:, h * TPC + col0: h * TPC + col0 + cols], in_=pD[0:1, 0:cols]),
                    waits=[("T", T_av)], inc=None)
                V_ev_h[bh] = vop(lambda e, h=h, cols=cols, col0=col0: e.tensor_copy(
                    out=att[:, h * TPC + col0: h * TPC + col0 + cols], in_=pC[:, 0:cols]))
            else:
                vop(lambda e, h=h, cols=cols, col0=col0: e.tensor_tensor(
                    out=lsl[:, h * TPC + col0: h * TPC + col0 + cols],
                    in0=lsl[:, h * TPC + col0: h * TPC + col0 + cols],
                    in1=pD[0:1, 0:cols], op=ALU.add), waits=[("T", T_av)], inc=None)
                V_ev_h[bh] = vop(lambda e, h=h, cols=cols, col0=col0: e.tensor_tensor(
                    out=att[:, h * TPC + col0: h * TPC + col0 + cols],
                    in0=att[:, h * TPC + col0: h * TPC + col0 + cols],
                    in1=pC[:, 0:cols], op=ALU.add))
            prev_bh = bh
        if b + 2 < NBLK:
            L_blk[b + 2] = kv_load(b + 2, b % 2, [("T", pl.c("T"))])
    V_attn = pl.c("V")
    if "att" in dbg:
        edma(dbg_outs["att"][:, :], att[:, :], waits=[("V", V_attn)])
    if "lsl" in dbg:
        edma(dbg_outs["lsl"][:, :], lsl[:, :], waits=[("V", V_attn)])
    if not want("wout"):
        return finish()

    V_rec = vop(lambda e: e.reciprocal(out=lsl[:, :], in_=lsl[:, :]), waits=[("V", V_attn)])
    L_xr = 0
    for k in range(KD):
        L_xr = ldma(x_rl[:, k * TPC:(k + 1) * TPC], x_in[:, k * 128:(k + 1) * 128], transpose=True,
                    waits=([("V", V_attn)] if k == 0 else ()), sem="Lx")
    V_nrm = V_rec
    for h in range(H):
        for hf in range(2):
            top(lambda e, h=h, hf=hf: e.matmul(
                pD[:, hf * 512:(hf + 1) * 512], onesf_col[0:1, 0:128],
                lsl[0:1, h * TPC + hf * 512: h * TPC + hf * 512 + 512], start=True, stop=True),
                waits=([("V", V_rec), ("V", V_nrm)] if hf == 0 else ()),
                inc=("T" if hf == 1 else None))
        T_n = pl.c("T")
        V_nrm = vop(lambda e, h=h: e.tensor_tensor(
            out=att_bf[:, h * TPC:(h + 1) * TPC], in0=att[:, h * TPC:(h + 1) * TPC],
            in1=pD[:, :], op=ALU.mult), waits=[("T", T_n)])
    V_nrmf = pl.c("V")
    wom = {}
    for og in range(8):
        buf = og % 2
        w = [("T", wom[("T", og - 2)])] if og >= 2 else []
        Lw = ldma(woutbuf[:, buf * 4096:(buf + 1) * 4096].rearrange("p (k o) -> p k o", o=256),
                  wout_in[:, og * 256:(og + 1) * 256].rearrange("(k p) o -> p k o", p=128),
                  waits=w, sem=f"Wq{buf}")
        ps_pair = (pA, pB) if og % 2 == 0 else (pC, pD)
        tw = [(f"Wq{buf}", Lw), ("V", V_nrmf)]
        if og >= 2:
            tw.append(("V", wom[("V", og - 2)]))
        first = True
        for k in range(KD):
            for oo in range(2):
                tgt = ps_pair[oo]
                for hf in range(2):
                    top(lambda e, b=buf, k=k, oo=oo, hf=hf, tgt=tgt: e.matmul(
                        tgt[:, hf * 512:(hf + 1) * 512],
                        woutbuf[:, b * 4096 + k * 256 + oo * 128: b * 4096 + k * 256 + oo * 128 + 128],
                        att_bf[:, k * TPC + hf * 512: k * TPC + hf * 512 + 512],
                        start=(k == 0), stop=(k == KD - 1)),
                        waits=(tw if first else ()),
                        inc=("T" if (k == KD - 1 and oo == 1 and hf == 1) else None))
                    first = False
        wom[("T", og)] = pl.c("T")
        Tog = pl.c("T")
        for oo in range(2):
            o = og * 2 + oo
            src = ps_pair[oo]
            vop(lambda e, o=o, src=src: e.tensor_scalar(
                out=csc2[:, :], in0=src[:, :],
                scalar1=boutv[:, o:o + 1], scalar2=dv[:, 32 + o:33 + o],
                op0=ALU.add, op1=ALU.mult), waits=[("T", Tog), ("Lx", L_xr)], inc=None)
            vop(lambda e, o=o: e.tensor_tensor(
                out=x1[:, o * TPC:(o + 1) * TPC], in0=csc2[:, :],
                in1=x_rl[:, o * TPC:(o + 1) * TPC], op=ALU.add))
        wom[("V", og)] = pl.c("V")
    V_x1 = pl.c("V")
    if "x1" in dbg:
        edma(dbg_outs["x1"][:, :], x1[:, :], waits=[("V", V_x1)])
    if not want("ln2"):
        return finish()

    for k in range(KD):
        V_c = vop(lambda e, k=k: e.tensor_copy(out=x1b[:, :], in_=x1[:, k * TPC:(k + 1) * TPC]),
                  waits=([("V", V_x1)] if k == 0 else [("T", pl.c("T"))]))
        S_sq = sop(lambda e: e.activation(out=sq2[:, :], in_=x1b[:, :], func=ACTF.Square),
                   waits=[("V", V_c)])
        for hf in range(2):
            top(lambda e, k=k, hf=hf: e.matmul(
                pB[0:1, hf * 512:(hf + 1) * 512], ones_bf[:, 0:1],
                x1b[:, hf * 512: hf * 512 + 512],
                start=(k == 0), stop=(k == KD - 1)),
                waits=([("V", V_c)] if hf == 0 else ()))
            top(lambda e, k=k, hf=hf: e.matmul(
                pC[0:1, hf * 512:(hf + 1) * 512], ones_bf[:, 0:1],
                sq2[:, hf * 512: hf * 512 + 512],
                start=(k == 0), stop=(k == KD - 1)),
                waits=([("S", S_sq)] if hf == 0 else ()),
                inc=("T" if hf == 1 else None))
    T_ln2 = pl.c("T")
    vop(lambda e: e.tensor_scalar(out=st1[:, :], in0=pB[0:1, :], scalar1=1.0 / D, scalar2=None, op0=ALU.mult),
        waits=[("T", T_ln2)], inc=None)
    vop(lambda e: e.tensor_scalar(out=st2[:, :], in0=pC[0:1, :], scalar1=1.0 / D, scalar2=None, op0=ALU.mult), inc=None)
    vop(lambda e: e.tensor_tensor(out=st3[:, :], in0=st1[:, :], in1=st1[:, :], op=ALU.mult), inc=None)
    vop(lambda e: e.tensor_tensor(out=st2[:, :], in0=st2[:, :], in1=st3[:, :], op=ALU.subtract), inc=None)
    V_var2 = vop(lambda e: e.tensor_scalar(out=st2[:, :], in0=st2[:, :], scalar1=1e-5, scalar2=None, op0=ALU.add))
    S_rs2 = sop(lambda e: e.activation(out=st2[:, :], in_=st2[:, :], func=ACTF.Sqrt), waits=[("V", V_var2)])
    vop(lambda e: e.reciprocal(out=st2[:, :], in_=st2[:, :]), waits=[("S", S_rs2)], inc=None)
    V_mr2 = vop(lambda e: e.tensor_tensor(out=st1[:, :], in0=st1[:, :], in1=st2[:, :], op=ALU.mult))
    for (ix, (dstb, srcv)) in enumerate(((abc, st2), (bbc, st1))):
        for hf in range(2):
            top(lambda e, s=srcv, hf=hf: e.matmul(
                pD[:, hf * 512:(hf + 1) * 512], onesf_col[0:1, 0:128],
                s[0:1, hf * 512: hf * 512 + 512], start=True, stop=True),
                waits=([("V", max(V_mr2, pl.c("V")))] if hf == 0 else ()),
                inc=("T" if hf == 1 else None))
        Tb = pl.c("T")
        vop(lambda e, d=dstb: e.tensor_copy(out=d[:, :], in_=pD[:, :]), waits=[("T", Tb)])
    V_bc2 = pl.c("V")
    V_mm = V_bc2
    for k in range(KD):
        vop(lambda e, k=k: e.tensor_tensor(out=dsc[:, :], in0=x1[:, k * TPC:(k + 1) * TPC], in1=abc[:, :],
                                           op=ALU.mult), inc=None)
        vop(lambda e: e.tensor_tensor(out=dsc[:, :], in0=dsc[:, :], in1=bbc[:, :], op=ALU.subtract), inc=None)
        vop(lambda e, k=k: e.tensor_scalar(out=h2sl[:, k * TPC:(k + 1) * TPC], in0=dsc[:, :],
                                           scalar1=dv[:, 16 + k:17 + k], scalar2=modv[:, 48 + k:49 + k],
                                           op0=ALU.mult, op1=ALU.add), inc=None)
        if k == 0:
            vop(lambda e: e.tensor_copy(out=mmax[:, :], in_=h2sl[:, 0:TPC]), inc=None)
            V_mm = vop(lambda e: e.tensor_copy(out=mmin[:, :], in_=h2sl[:, 0:TPC]))
        else:
            vop(lambda e, k=k: e.tensor_tensor(out=mmax[:, :], in0=mmax[:, :],
                                               in1=h2sl[:, k * TPC:(k + 1) * TPC], op=ALU.max), inc=None)
            V_mm = vop(lambda e, k=k: e.tensor_tensor(out=mmin[:, :], in0=mmin[:, :],
                                                      in1=h2sl[:, k * TPC:(k + 1) * TPC], op=ALU.min))
    V_am = vop(lambda e: e.scalar_tensor_tensor(out=mmax[:, :], in0=mmin[:, :], scalar=-1.0, in1=mmax[:, :],
                                                op0=ALU.mult, op1=ALU.max), waits=[("V", V_mm)])

    def token_scales(src_mm, V_ready, dst_s2bc, dst_ub, ub_scale_col):
        for i in range(NT):
            top(lambda e, i=i, s=src_mm: e.transpose(pA[:, i * 128:(i + 1) * 128],
                                                     s[:, i * 128:(i + 1) * 128], id_f[:, :]),
                waits=([("V", V_ready)] if i == 0 else ()), inc=("T" if i == NT - 1 else None))
        T_amT = pl.c("T")
        for i in range(NT):
            vop_s(lambda e, i=i: e.reduce_max(out=stok[:, i:i + 1], in_=pA[:, i * 128:(i + 1) * 128], axis=AXX),
                  waits=([("T", T_amT)] if i == 0 else ()))
        vop_s(lambda e: e.tensor_scalar(out=stok[:, 0:8], in0=stok[:, 0:8], scalar1=1e-5, scalar2=None,
                                        op0=ALU.max))
        vop_s(lambda e: e.tensor_scalar(out=stok[:, 8:16], in0=stok[:, 0:8], scalar1=1.0 / 127.0, scalar2=None,
                                        op0=ALU.mult))
        vop_s(lambda e: e.reciprocal(out=stok[:, 0:8], in_=stok[:, 0:8]))
        V_s2 = vop_s(lambda e: e.tensor_scalar(out=stok[:, 0:8], in0=stok[:, 0:8], scalar1=127.0, scalar2=None,
                                               op0=ALU.mult))
        T_st = top(lambda e: e.transpose(pB[0:16, 0:128], stok[:, 0:16], id_f[:, :]),
                   waits=[("V", V_s2)], inc="T")
        V_st = vop(lambda e: e.tensor_copy(out=stokT[:, :], in_=pB[0:16, 0:128]), waits=[("T", T_st)])
        ldma(s2rowA[0:1, :].rearrange("o (i c) -> o i c", c=128), stokT[0:8, :],
             waits=[("V", V_st), ("T", pl.c("T"))] + [(f"Fo{i}", pl.c(f"Fo{i}")) for i in range(4)],
             sem="Lc")
        L_s2r = ldma(s2rowB[0:1, :].rearrange("o (i c) -> o i c", c=128), stokT[8:16, :], sem="Lc")
        for i in range(NT):
            top(lambda e, i=i: e.matmul(pC[:, i * 128:(i + 1) * 128], onesf_col[0:1, 0:128],
                                        s2rowA[0:1, i * 128:(i + 1) * 128], start=True, stop=True),
                waits=([("Lc", L_s2r)] if i == 0 else ()), inc=None)
            top(lambda e, i=i: e.matmul(pD[:, i * 128:(i + 1) * 128], onesf_col[0:1, 0:128],
                                        s2rowB[0:1, i * 128:(i + 1) * 128], start=True, stop=True),
                inc=("T" if i == NT - 1 else None))
        T_bc = pl.c("T")
        vop(lambda e, d=dst_s2bc: e.tensor_copy(out=d[:, :], in_=pC[:, 0:TPC]), waits=[("T", T_bc)], inc=None)
        marks = []
        for (dub, cix) in dst_ub:
            marks.append(vop(lambda e, d=dub, cx=cix: e.tensor_scalar(
                out=d[:, :], in0=pD[:, 0:TPC], scalar1=sclc[:, cx:cx + 1], scalar2=None, op0=ALU.mult)))
        return marks[-1], pl.c("V")

    V_ub, _ = token_scales(mmax, V_am, s2bc, ((ubg, 4), (ubu, 5)), None)
    if "s2" in dbg:
        edma(dbg_outs["s2"][:, :], stok[:, :], waits=[("V", V_ub)])
    V_xq = V_ub
    for k in range(KD):
        vop(lambda e, k=k: e.tensor_tensor(out=dsc[:, :], in0=h2sl[:, k * TPC:(k + 1) * TPC], in1=s2bc[:, :],
                                           op=ALU.mult), waits=([("V", V_ub)] if k == 0 else ()), inc=None)
        vop(lambda e: e.tensor_scalar(out=dsc[:, :], in0=dsc[:, :], scalar1=MAGIC, scalar2=-MAGIC,
                                      op0=ALU.add, op1=ALU.add), inc=None)
        V_xq = vop(lambda e, k=k: e.tensor_scalar(out=xq[:, k * TPC:(k + 1) * TPC], in0=dsc[:, :],
                                                  scalar1=127.0, scalar2=-128.0, op0=ALU.min, op1=ALU.max))
    if "xq" in dbg:
        edma(dbg_outs["xq"][:, :], xq[:, :], waits=[("V", V_xq)])
    if not want("ffn1"):
        return finish()

    wagp = wq_ag_d.ap()
    fvm = {}
    F_mid = {}
    fc = 0
    for r in range(NC):
        for fs in range(2):
            buf = (r * 2 + fs) % 2
            si = r * 2 + fs
            w = [("V", fvm[4 * (si - 2) + 3])] if si >= 2 else [("Cw", C_wq), ("V", V_xq)]
            fdma(wslab[:, buf * 2 * 16 * 512: buf * 2 * 16 * 512 + 16 * 512],
                 wagp[r, 0:D * TPC].rearrange("(dd p f) -> p dd f", p=128, f=TPC)[:, :, fs * 512:(fs + 1) * 512],
                 waits=w, sem=f"Fw{buf}")
            F_sl = fdma(wslab[:, buf * 2 * 16 * 512 + 16 * 512: (buf + 1) * 2 * 16 * 512],
                        wagp[r, D * TPC:2 * D * TPC].rearrange("(dd p f) -> p dd f", p=128, f=TPC)[:, :, fs * 512:(fs + 1) * 512],
                        sem=f"Fw{buf}")
            for fl in range(4):
                pg = pA if fl % 2 == 0 else pC
                pu = pB if fl % 2 == 0 else pD
                tw = [(f"Fw{buf}", F_sl), ("V", V_xq)]
                if fc >= 2:
                    tw.append(("V", fvm[fc - 2]))
                first = True
                gb0 = buf * 2 * 16 * 512
                ub0 = gb0 + 16 * 512
                for k in range(KD):
                    for (tgt, base) in ((pg, gb0), (pu, ub0)):
                        for hf in range(2):
                            top(lambda e, tgt=tgt, base=base, k=k, fl=fl, hf=hf: e.matmul(
                                tgt[:, hf * 512:(hf + 1) * 512],
                                wslab[:, base + k * 512 + fl * 128: base + k * 512 + (fl + 1) * 128],
                                xq[:, k * TPC + hf * 512: k * TPC + hf * 512 + 512],
                                start=(k == 0), stop=(k == KD - 1)),
                                waits=(tw if first else ()),
                                inc=("T" if (k == KD - 1 and tgt is pu and hf == 1) else None))
                            first = False
                T_fl = pl.c("T")
                mb = fc % 4
                mw = [("T", T_fl)]
                if fc >= 4:
                    mw.append((f"Fo{mb}", F_mid[fc - 4]))
                V_g1 = vop(lambda e, pg=pg: e.tensor_tensor(out=gsc1[:, :], in0=pg[:, :], in1=ubg[:, :], op=ALU.mult),
                           waits=mw)
                S_sl = sop(lambda e: e.activation(out=gsc2[:, :], in_=gsc1[:, :], func=ACTF.Silu),
                           waits=[("V", V_g1)])
                vop(lambda e, pu=pu: e.tensor_tensor(out=gsc1[:, :], in0=pu[:, :], in1=ubu[:, :], op=ALU.mult),
                    waits=[("S", S_sl)], inc=None)
                vop(lambda e, mb=mb: e.tensor_tensor(out=midb[:, mb * TPC:(mb + 1) * TPC], in0=gsc2[:, :],
                                                     in1=gsc1[:, :], op=ALU.mult), inc=None)
                if fc == 0:
                    vop(lambda e: e.tensor_copy(out=fmx[:, :], in_=midb[:, 0:TPC]), inc=None)
                    fvm[fc] = vop(lambda e: e.tensor_copy(out=fmn[:, :], in_=midb[:, 0:TPC]))
                else:
                    vop(lambda e, mb=mb: e.tensor_tensor(out=fmx[:, :], in0=fmx[:, :],
                                                         in1=midb[:, mb * TPC:(mb + 1) * TPC], op=ALU.max), inc=None)
                    fvm[fc] = vop(lambda e, mb=mb: e.tensor_tensor(out=fmn[:, :], in0=fmn[:, :],
                                                                   in1=midb[:, mb * TPC:(mb + 1) * TPC], op=ALU.min))
                F_mid[fc] = fdma(mid_d[fc], midb[:, mb * TPC:(mb + 1) * TPC], waits=[("V", fvm[fc])], sem=f"Fo{mb}")
                fc += 1
    V_f1 = pl.c("V")
    if "mid" in dbg:
        for u in range(KF):
            gdma(dbg_outs["mid"][u * 128:(u + 1) * 128, :], mid_d[u],
                 waits=[(f"Fo{i}", pl.c(f"Fo{i}")) for i in range(4)])
    if not want("out"):
        return finish()

    V_am2 = vop(lambda e: e.scalar_tensor_tensor(out=fmx[:, :], in0=fmn[:, :], scalar=-1.0, in1=fmx[:, :],
                                                 op0=ALU.mult, op1=ALU.max), waits=[("V", V_f1)])
    V_ubd, _ = token_scales(fmx, V_am2, s2bc, ((ubd, 6),), None)

    F_mq = {}
    pvm = {}
    import os as _os
    NPASS = int(_os.environ.get("KNPASS", "4"))
    for q in range(NPASS):
        F_wd = {}
        F_wd[0] = fdma(wdslab[:, 0:8 * 512],
                       wagp[0, 2 * D * TPC:].rearrange("(ff p m) -> p ff m", p=128, m=D)[:, :, q * 512:(q + 1) * 512],
                       waits=[("T", pl.c("T")), ("V", pl.c("V"))], sem="Fw0")
        for fci in range(KF):
            r = fci // 8
            ffl = fci % 8
            mb = fci % 4
            qb = fci % 2
            if ffl == 0 and r + 1 < NC:
                F_wd[r + 1] = fdma(wdslab[:, ((r + 1) % 2) * 8 * 512:(((r + 1) % 2) + 1) * 8 * 512],
                                   wagp[r + 1, 2 * D * TPC:].rearrange("(ff p m) -> p ff m", p=128, m=D)[:, :, q * 512:(q + 1) * 512],
                                   waits=[("T", pl.c("T"))], sem=f"Fw{(r + 1) % 2}")
            if q == 0:
                wv = []
                if fci == 0:
                    wv += [(f"Fo{i}", pl.c(f"Fo{i}")) for i in range(4)]
                if fci >= 2:
                    wv.append(("V", fvm.get(("q0", fci - 2), 0)))
                Fm = fdma(midb[:, mb * TPC:(mb + 1) * TPC], mid_d[fci], waits=wv, sem=f"Fm{fci % 2}")
                vop(lambda e, mb=mb: e.tensor_tensor(out=fsc1[:, :], in0=midb[:, mb * TPC:(mb + 1) * TPC],
                                                     in1=s2bc[:, :], op=ALU.mult),
                    waits=[(f"Fm{fci % 2}", Fm), ("V", V_ubd)], inc=None)
                vop(lambda e: e.tensor_scalar(out=fsc1[:, :], in0=fsc1[:, :], scalar1=MAGIC, scalar2=-MAGIC,
                                              op0=ALU.add, op1=ALU.add), inc=None)
                wv2 = [("T", pl.c("T"))] if fci >= 2 else []
                if fci >= 2:
                    wv2.append((f"Fq{fci % 2}", F_mq[fci - 2]))
                V_q = vop(lambda e, qb=qb: e.tensor_scalar(out=midqb[:, qb * TPC:(qb + 1) * TPC], in0=fsc1[:, :],
                                                           scalar1=127.0, scalar2=-128.0, op0=ALU.min, op1=ALU.max),
                          waits=wv2)
                fvm[("q0", fci)] = V_q
                F_mq[fci] = fdma(midq_d[fci], midqb[:, qb * TPC:(qb + 1) * TPC], waits=[("V", V_q)], sem=f"Fq{fci % 2}")
                F_rdy = ("V", V_q)
            else:
                wv = [("T", pl.c("T"))] if fci >= 2 else []
                if fci == 0:
                    wv.append(("Fq0", pl.c("Fq0")))
                    wv.append(("Fq1", pl.c("Fq1")))
                F_rdy = (f"Fm{fci % 2}",
                         fdma(midqb[:, qb * TPC:(qb + 1) * TPC], midq_d[fci], waits=wv, sem=f"Fm{fci % 2}"))
            first = True
            for ol in range(4):
                tgt = pp[ol]
                for hf in range(2):
                    tw = []
                    if first:
                        tw = [F_rdy, (f"Fw{r % 2}", F_wd[r])]
                        if q == 0:
                            tw.append(("V", fvm[("q0", fci)]))
                        if fci == 0:
                            tw.append(("V", pvm.get(("Vq", q - 1), 0)))
                    top(lambda e, r=r, ffl=ffl, ol=ol, hf=hf, tgt=tgt, qb=qb, fci=fci: e.matmul(
                        tgt[:, hf * 512:(hf + 1) * 512],
                        wdslab[:, (r % 2) * 8 * 512 + ffl * 512 + ol * 128: (r % 2) * 8 * 512 + ffl * 512 + (ol + 1) * 128],
                        midqb[:, qb * TPC + hf * 512: qb * TPC + hf * 512 + 512],
                        start=(fci == 0), stop=(fci == KF - 1)),
                        waits=tw,
                        inc=("T" if (ol == 3 and hf == 1) else None))
                    first = False
        T_q = pl.c("T")
        pvm[("Tq", q)] = T_q
        for ol in range(4):
            o = q * 4 + ol
            vop(lambda e, ol=ol: e.tensor_tensor(out=fsc1[:, :], in0=pp[ol][:, :], in1=ubd[:, :], op=ALU.mult),
                waits=[("T", T_q)], inc=None)
            V_x2 = vop(lambda e, o=o: e.scalar_tensor_tensor(
                out=fsc2[:, :], in0=fsc1[:, :], scalar=dv[:, 48 + o:49 + o],
                in1=x1[:, o * TPC:(o + 1) * TPC], op0=ALU.mult, op1=ALU.add))
            for i in range(NT):
                top(lambda e, ol=ol, i=i: e.transpose(pp[ol][:, i * 128:(i + 1) * 128],
                                                      fsc2[:, i * 128:(i + 1) * 128], id_f[:, :]),
                    waits=([("V", V_x2)] if i == 0 else ()), inc=("T" if i == NT - 1 else None))
            T_tr = pl.c("T")
            vop(lambda e, ol=ol, o=o: e.tensor_copy(
                out=x2tm[:, :].rearrange("p (i dcol) -> p i dcol", dcol=D)[:, :, o * 128:(o + 1) * 128],
                in_=pp[ol][:, :].rearrange("p (i qq) -> p i qq", qq=128)), waits=[("T", T_tr)])
        pvm[("Vq", q)] = pl.c("V")
    V_x2f = pl.c("V")
    for i in range(NT):
        edma(out_d[i * 128:(i + 1) * 128, :], x2tm[:, i * D:(i + 1) * D],
             waits=([("V", V_x2f)] if i == 0 else ()), sem="Oo")
    return finish()


def _emit(nc, pl, perm):
    from contextlib import ExitStack
    names = sorted(set(pl.cnt.keys()) | {"T", "V", "S"})
    with ExitStack() as st_, nc.Block() as block:
        sems = {n: st_.enter_context(nc.semaphore("sem_" + n)) for n in names}

        @block.sync
        def _(e):
            for fn in pl.ops["sync"]:
                fn(e, sems)

        @block.tensor
        def _(e):
            for fn in pl.ops["tensor"]:
                fn(e, sems)

        @block.vector
        def _(e):
            for fn in pl.ops["vector"]:
                fn(e, sems)

        @block.scalar
        def _(e):
            for fn in pl.ops["scalar"]:
                fn(e, sems)

        @block.gpsimd
        def _(e):
            for fn in pl.ops["gpsimd"]:
                fn(e, sems)

    perm.close()
    return nc


def make_consts():
    bf = ml_dtypes.bfloat16
    return {
        "ones_bf": np.ones((128, 128), bf),
        "id_bf": np.eye(128, dtype=np.float32).astype(bf),
        "id_f": np.eye(128, dtype=np.float32),
    }


def make_masks(c):
    bf = ml_dtypes.bfloat16
    tiles = tiles_of_core(c)
    m = np.zeros((8, 8, 128, 128), np.float32)
    tri = np.tril(np.ones((128, 128), np.float32)).T
    for j in range(8):
        g = tiles[j]
        for u in range(8):
            b = 4 * j + u // 2
            s = u % 2
            t = 2 * b + s
            if t < g:
                m[j, u] = 1.0
            elif t == g:
                m[j, u] = tri
    return m.reshape(8 * 8 * 128, 128).astype(bf)


def host_inputs(inputs):
    x = np.asarray(inputs["L0_attn_x"]).reshape(S, D)
    wqkv = np.ascontiguousarray(np.asarray(inputs["L0_attn_W_qkv"]))
    bq = np.ascontiguousarray(np.asarray(inputs["L0_attn_b_qkv"]))
    wo = np.ascontiguousarray(np.asarray(inputs["L0_attn_W_out"]))
    bo = np.ascontiguousarray(np.asarray(inputs["L0_attn_b_out"]))
    wg = np.asarray(inputs["L0_ffn_w_gate"], np.float32)
    wu = np.asarray(inputs["L0_ffn_w_up"], np.float32)
    wd = np.asarray(inputs["L0_ffn_w_down"], np.float32)
    cvec = np.ascontiguousarray(np.asarray(inputs["L0_c"], np.float32).reshape(D))
    wmod = np.asarray(inputs["L0_wmod"], np.float32)
    consts = make_consts()
    ims = []
    for c in range(NC):
        tiles = tiles_of_core(c)
        xs = np.ascontiguousarray(np.concatenate([x[g * 128:(g + 1) * 128] for g in tiles], axis=0))
        im = {
            "x": xs,
            "wqkv": wqkv, "bqkv": bq, "wout": wo, "bout": bo,
            "cvec": cvec,
            "wmods": np.ascontiguousarray(wmod[:, c * WM_SL:(c + 1) * WM_SL]),
            "wgs": np.ascontiguousarray(wg[c * TPC:(c + 1) * TPC, :]),
            "wus": np.ascontiguousarray(wu[c * TPC:(c + 1) * TPC, :]),
            "wds": np.ascontiguousarray(wd[:, c * TPC:(c + 1) * TPC]),
            "masks": make_masks(c),
        }
        im.update(consts)
        ims.append(im)
    return ims


def kernel(**inputs):
    ims = host_inputs(inputs)
    nc = build()
    res = run_bass_kernel_spmd(nc, ims, core_ids=list(range(NC)))
    out = np.empty((S, D), np.float32)
    for c in range(NC):
        o = res.results[c]["out"]
        for j, g in enumerate(tiles_of_core(c)):
            out[g * 128:(g + 1) * 128] = o[j * 128:(j + 1) * 128]
    return out.reshape(1, S, D)
